# version 01
# baseline (speedup 1.0000x reference)
import math
import os
from contextlib import ExitStack

import numpy as np
import ml_dtypes

import concourse.bass as bass
import concourse.mybir as mybir
from concourse.bass_utils import run_bass_kernel_spmd

F32 = mybir.dt.float32
BF16 = mybir.dt.bfloat16
I16 = mybir.dt.int16
I32 = mybir.dt.int32
U8 = mybir.dt.uint8
OP = mybir.AluOpType
AF = mybir.ActivationFunctionType
AX = mybir.AxisListType

P = 128
E = 2048
KT = E // P
TOK = 1024
T4 = 4096
NEG = -3.4028234663852886e38
EPS = 1e-5
RSD0 = 1.0 / math.sqrt(128.0)
RSD1 = 1.0 / math.sqrt(64.0)
LAMBDA_INIT = 0.8 - 0.6 * math.exp(-0.3 * 12)
BF = ml_dtypes.bfloat16
GROUPS = [[0, 1, 2, 3], [4, 5, 6, 7]]

ENGINES = ("sync", "tensor", "vector", "scalar", "gpsimd")



class _PhaseDone(Exception):
    pass


class Sched:

    def __init__(self, nc, ctx):
        self.nc = nc
        self.sems = {e: ctx.enter_context(nc.semaphore(f"s_{e}"))
                     for e in ENGINES}
        self.bars = ctx.enter_context(nc.semaphore("s_bar"))
        self.ops = {e: [] for e in ENGINES}
        self.events = {}
        self.counts = {e: 0 for e in ENGINES}
        self.nbar = 0

    def add(self, engine, fn, deps=(), event=None, dma=False, coll=False,
            self_inc=0):
        inc = self_inc or (16 if dma else (1 if (event or coll) else 0))
        if inc:
            self.counts[engine] += inc
        if event is not None:
            assert event not in self.events, f"dup event {event}"
            self.events[event] = (engine, self.counts[engine])
        self.ops[engine].append((fn, tuple(deps), 0 if self_inc else inc, inc))

    def sem(self, engine):
        return self.sems[engine]

    def barrier(self):
        self.nbar += 1
        for e in ENGINES:
            self.ops[e].append((("bar", self.nbar), (), 0, 0))

    def fence_all(self):
        finals = dict(self.counts)
        for e in ENGINES:
            def mk(e=e, finals=finals):
                def f(eng):
                    for e2 in ENGINES:
                        if finals[e2]:
                            eng.wait_ge(self.sems[e2], finals[e2])
                    return None
                return f
            self.ops[e].append((mk(), (), 0, 0))

    def check(self):
        ptr = {e: 0 for e in ENGINES}
        done = set()
        barwait = {e: 0 for e in ENGINES}
        counts = {e: 0 for e in ENGINES}
        incs = {}
        for e in ENGINES:
            lst = []
            c = 0
            for (fn, deps, inc, sim_inc) in self.ops[e]:
                if isinstance(fn, tuple):
                    lst.append(("bar", fn[1], 0))
                else:
                    lst.append(("op", deps, sim_inc))
            incs[e] = lst
        progress = True
        nbar_hit = {e: 0 for e in ENGINES}
        while progress:
            progress = False
            at_bar = [e for e in ENGINES
                      if ptr[e] < len(incs[e]) and incs[e][ptr[e]][0] == "bar"]
            if len(at_bar) == len([e for e in ENGINES
                                   if ptr[e] < len(incs[e])]) and at_bar:
                bar_ids = {incs[e][ptr[e]][1] for e in at_bar}
                if len(bar_ids) == 1:
                    for e in at_bar:
                        ptr[e] += 1
                    progress = True
                    continue
            for e in ENGINES:
                while ptr[e] < len(incs[e]):
                    kind = incs[e][ptr[e]][0]
                    if kind == "bar":
                        break
                    _, deps, inc = incs[e][ptr[e]]
                    ok = True
                    for d in deps:
                        src_e, cnt_d = self.events[d]
                        if src_e != e and counts[src_e] < cnt_d:
                            ok = False
                            break
                    if not ok:
                        break
                    counts[e] += inc
                    ptr[e] += 1
                    progress = True
        stuck = {e: ptr[e] for e in ENGINES if ptr[e] < len(incs[e])}
        if stuck:
            detail = {}
            for e, i in stuck.items():
                if incs[e][i][0] == "op":
                    blocked = [d for d in incs[e][i][1]
                               if self.events[d][0] != e and
                               counts[self.events[d][0]] <
                               self.events[d][1]]
                    detail[e] = (i, blocked)
                else:
                    detail[e] = (i, "barrier")
            raise RuntimeError(f"Sched deadlock: {detail}")

    def emit(self, block):
        self.check()
        def run(eng, ename):
            waited = {}
            for fn, deps, inc, _sim in self.ops[ename]:
                if isinstance(fn, tuple) and fn[0] == "bar":
                    eng.sem_inc(self.bars, 1)
                    eng.wait_ge(self.bars, 5 * fn[1])
                    continue
                for d in deps:
                    src, cnt = self.events[d]
                    if src == ename or waited.get(src, -1) >= cnt:
                        continue
                    eng.wait_ge(self.sems[src], cnt)
                    waited[src] = cnt
                inst = fn(eng)
                if inc:
                    assert inst is not None, "event op returned no instruction"
                    inst.then_inc(self.sems[ename], inc)

        @block.sync
        def _(sync):
            run(sync, "sync")

        @block.tensor
        def _(tensor):
            run(tensor, "tensor")

        @block.vector
        def _(vector):
            run(vector, "vector")

        @block.scalar
        def _(scalar):
            run(scalar, "scalar")

        @block.gpsimd
        def _(gpsimd):
            run(gpsimd, "gpsimd")



def _fm(a):
    return np.ascontiguousarray(np.asarray(a).T)


def _u8view(a_int32):
    a = np.ascontiguousarray(np.asarray(a_int32, dtype=np.int32))
    return np.ascontiguousarray(a.view(np.uint8)[..., ::4])


def _consts():
    ident = np.eye(P, dtype=BF)
    identf = np.eye(P, dtype=np.float32)
    ctile = np.where(np.arange(P)[None, :] <= np.arange(P)[:, None],
                     np.float32(0.0), np.float32(NEG)).astype(np.float32)
    posv = np.arange(T4, dtype=np.float32)[None, :]
    jv = np.concatenate([np.arange(32, dtype=np.float32)] * 2)[:, None]
    return dict(ident=ident, identf=identf, ctile=ctile, posv=posv, jv=jv)


def prep_l0attn(inputs):
    x = np.asarray(inputs["L0_attn_x"])
    gq = np.asarray(inputs["L0_attn_g_q"])
    gkv = np.asarray(inputs["L0_attn_g_kv"])
    wqa = np.asarray(inputs["L0_attn_w_qa"])
    wqb = np.asarray(inputs["L0_attn_w_qb"])
    wkva = np.asarray(inputs["L0_attn_w_kva"])
    wkvb = np.asarray(inputs["L0_attn_w_kvb"])
    wo = np.asarray(inputs["L0_attn_w_o"])
    C = _consts()
    gq_t = np.ascontiguousarray(gq.reshape(KT, P).T).astype(np.float32)
    gkv_t = np.ascontiguousarray(gkv.reshape(KT, P).T).astype(np.float32)
    wkva_kc = np.ascontiguousarray(wkva[:, 0:128]).astype(np.float32)
    maps = []
    for c in range(8):
        b, u = c // 4, c % 4
        heads = range(4 * u, 4 * u + 4)
        vcols = np.concatenate(
            [wkva[:, 128 + 256 * h:128 + 256 * h + 128] for h in heads],
            axis=1)
        maps.append({
            "x": _fm(x[b, 1024 * u:1024 * u + 1024, :]).astype(np.float32),
            "gq": gq_t, "gkv": gkv_t,
            "wqa": np.ascontiguousarray(wqa).astype(np.float32),
            "wqb": np.ascontiguousarray(
                wqb[:, 512 * u:512 * u + 512]).astype(np.float32),
            "wkva_kc": wkva_kc,
            "wkva_v": np.ascontiguousarray(vcols).astype(np.float32),
            "wkvb": np.ascontiguousarray(
                wkvb[:, 512 * u:512 * u + 512]).astype(np.float32),
            "wo": np.ascontiguousarray(wo).astype(np.float32),
            "posv": C["posv"], "jv": C["jv"], "ctile": C["ctile"],
            "ident": C["ident"], "identf": C["identf"],
        })
    return maps



def emit_ln(s, tag, n_tok, x_tile, x_is_f32, sqb, rows, ps_s, ps_q,
            ones_f, ones_b, eps_ap, deps):
    nh = n_tok // 512

    def slot(kt):
        return sqb[:, (kt % 2) * n_tok:(kt % 2) * n_tok + n_tok]

    for kt in range(KT):
        dd = list(deps) + ([f"{tag}.mm{kt-2}"] if kt >= 2 else [])
        s.add("scalar",
              (lambda kt=kt: (lambda eng: eng.activation(
                  slot(kt), x_tile(kt), AF.Square)))(),
              deps=dd, event=f"{tag}.sq{kt}")

        def mk(kt=kt):
            def f(eng):
                last = None
                for th in range(nh):
                    sl = slice(512 * th, 512 * th + 512)
                    eng.matmul(ps_s[0:1, sl], ones_f if x_is_f32 else ones_b,
                               x_tile(kt)[:, sl],
                               start=(kt == 0), stop=(kt == KT - 1))
                    last = eng.matmul(ps_q[0:1, sl], ones_f, slot(kt)[:, sl],
                                      start=(kt == 0), stop=(kt == KT - 1))
                return last
            return f
        s.add("tensor", mk(), deps=[f"{tag}.sq{kt}"] + list(deps),
              event=f"{tag}.mm{kt}")

    mu = rows[0:1, 0:n_tok]
    ex2 = rows[0:1, n_tok:2 * n_tok]
    var = rows[0:1, 2 * n_tok:3 * n_tok]
    stdv = rows[0:1, 3 * n_tok:4 * n_tok]
    rstd = rows[0:1, 4 * n_tok:5 * n_tok]
    nmur = rows[0:1, 5 * n_tok:6 * n_tok]

    def rows1(eng):
        eng.tensor_single_scalar(mu, ps_s[0:1, 0:n_tok], 1.0 / E, OP.mult)
        eng.tensor_single_scalar(ex2, ps_q[0:1, 0:n_tok], 1.0 / E, OP.mult)
        eng.tensor_tensor(var, mu, mu, OP.mult)
        return eng.tensor_tensor(var, ex2, var, OP.subtract)
    s.add("vector", rows1, deps=[f"{tag}.mm{KT-1}"], event=f"{tag}.var")
    s.add("scalar", lambda eng: eng.activation(
        stdv, var, AF.Sqrt, bias=eps_ap, scale=1.0),
        deps=[f"{tag}.var"], event=f"{tag}.std")

    def rows2(eng):
        eng.reciprocal(rstd, stdv)
        return eng.scalar_tensor_tensor(nmur, mu, -1.0, rstd, OP.mult, OP.mult)
    s.add("vector", rows2, deps=[f"{tag}.std"], event=f"{tag}.rows")
    return rstd, nmur


def emit_ln_apply(s, tag, n_tok, x_tile, out_tile, rstd, nmur, ps_bc, ones_rf,
                  asc, deps):
    nh = n_tok // 512
    for th in range(nh):
        def mk_bc(th=th):
            def f(eng):
                eng.matmul(ps_bc[:, 0:512], ones_rf,
                           rstd[0:1, 512 * th:512 * th + 512],
                           start=True, stop=True)
                return eng.matmul(ps_bc[:, 512:1024], ones_rf,
                                  nmur[0:1, 512 * th:512 * th + 512],
                                  start=True, stop=True)
            return f
        dd = list(deps) + [f"{tag}.rows"] + ([f"{tag}.ap{th-1}"] if th else [])
        s.add("tensor", mk_bc(), deps=dd, event=f"{tag}.bc{th}")

        def mk_ap(th=th):
            def f(eng):
                last = None
                for kt in range(KT):
                    xs = x_tile(kt)[:, 512 * th:512 * th + 512]
                    os = out_tile(kt)[:, 512 * th:512 * th + 512]
                    eng.tensor_tensor(asc[:, 0:512], xs, ps_bc[:, 0:512],
                                      OP.mult)
                    last = eng.tensor_tensor(os, asc[:, 0:512],
                                             ps_bc[:, 512:1024], OP.add)
                return last
            return f
        s.add("vector", mk_ap(), deps=[f"{tag}.bc{th}"], event=f"{tag}.ap{th}")
    s.add("vector", lambda eng: eng.engine_nop(),
          deps=[f"{tag}.ap{t}" for t in range(nh)], event=f"{tag}.done")



def build_l0attn():
    PH = int(os.environ.get("L0PH", "99"))
    P2L = int(os.environ.get("L0P2", "9"))
    NHT = int(os.environ.get("L0NHT", "8"))
    NODUMP = os.environ.get("L0NODUMP", "")
    nc = bass.Bass("TRN2", target_bir_lowering=False)
    D = {}
    for name, shape, dt in [
        ("x", [E, TOK], F32), ("gq", [P, KT], F32), ("gkv", [P, KT], F32),
        ("wqa", [E, 64], F32), ("wqb", [64, 512], F32),
        ("wkva_kc", [E, 128], F32), ("wkva_v", [E, 512], F32),
        ("wkvb", [128, 512], F32), ("wo", [E, E], F32),
        ("posv", [1, T4], F32), ("jv", [64, 1], F32),
        ("ctile", [P, P], F32), ("ident", [P, P], BF16),
        ("identf", [P, P], F32),
    ]:
        D[name] = nc.dram_tensor(name, shape, dt, kind="ExternalInput").ap()
    xo = nc.dram_tensor("xo", [E, TOK], F32, kind="ExternalOutput").ap()
    agi = nc.dram_tensor("agi", [E, TOK], BF16)
    ago = nc.dram_tensor("ago", [4 * E, TOK], BF16)
    ygi = nc.dram_tensor("ygi", [512, T4], BF16)
    ygo = nc.dram_tensor("ygo", [E, T4], BF16)

    with ExitStack() as top:
        s = Sched(nc, top)
        sb = lambda n_, sh, dt: top.enter_context(nc.sbuf_tensor(n_, sh, dt))

        ident = sb("sb_ident", [P, P], BF16)
        identf = sb("sb_identf", [P, P], F32)
        ctile = sb("sb_ctile", [P, P], F32)
        ones_cb = sb("ones_cb", [P, 1], BF16)
        ones_cf = sb("ones_cf", [P, 1], F32)
        ones_rf = sb("ones_rf", [1, P], F32)
        eps_c = sb("eps_c", [P, 1], F32)
        wqa_bf = sb("wqa_bf", [P, KT * 64], BF16)
        wqb_bf = sb("wqb_bf", [64, 512], BF16)
        wkc_bf = sb("wkc_bf", [P, KT * 128], BF16)
        wv_bf = sb("wv_bf", [P, KT * 512], BF16)
        wkvb_bf = sb("wkvb_bf", [P, 512], BF16)

        s.add("vector", lambda eng: (
            eng.memset(ones_cb[:], 1.0), eng.memset(ones_cf[:], 1.0),
            eng.memset(ones_rf[:], 1.0), eng.memset(eps_c[:], EPS),
        )[-1], event="consts")

        def ld(name, out_ap, in_ap, deps=()):
            s.add("sync", (lambda o=out_ap, i=in_ap:
                           (lambda eng: eng.dma_start(out=o, in_=i)))(),
                  deps=deps, event=name, dma=True)

        with ExitStack() as ph:
            sbp = lambda n_, sh, dt: ph.enter_context(nc.sbuf_tensor(n_, sh, dt))
            x_sb = sbp("x_sb", [P, KT * TOK], F32)
            wst = sbp("wst", [P, 4096], F32)
            gq_sb = sbp("gq_sb", [P, KT], F32)
            gkv_sb = sbp("gkv_sb", [P, KT], F32)
            sqb = sbp("sqb", [P, 2 * TOK], F32)
            rows = sbp("rows", [1, 6 * TOK], F32)
            xn = sbp("xn", [P, KT * TOK], BF16)
            yh = sbp("yh_own", [P, KT * TOK], BF16)
            asc = sbp("asc", [P, 512], F32)
            with ExitStack() as php:
                psp = lambda n_, sh: php.enter_context(
                    nc.psum_tensor(n_, sh, F32))
                ps_s = psp("ps_s", [1, TOK])
                ps_q = psp("ps_q", [1, TOK])
                ps_bc = psp("ps_bc", [P, 1024])

                ld("ld.x", x_sb[:].rearrange("p (kt t) -> p kt t", t=TOK),
                   D["x"].rearrange("(kt p) t -> p kt t", p=P))
                ld("ld.ident", ident[:], D["ident"])
                ld("ld.identf", identf[:], D["identf"])
                ld("ld.ctile", ctile[:], D["ctile"])
                ld("ld.gq", gq_sb[:], D["gq"])
                ld("ld.gkv", gkv_sb[:], D["gkv"])
                ld("ld.wqa", wst[:, 0:1024].rearrange(
                    "p (kt n) -> p kt n", n=64),
                   D["wqa"].rearrange("(kt p) n -> p kt n", p=P))
                ld("ld.wkc", wst[:, 1024:3072].rearrange(
                    "p (kt n) -> p kt n", n=128),
                   D["wkva_kc"].rearrange("(kt p) n -> p kt n", p=P))
                ld("ld.wqb", wst[0:64, 3072:3584], D["wqb"])
                ld("ld.wkvb", wst[:, 3584:4096], D["wkvb"])

                def fold_w(eng):
                    for kt in range(KT):
                        eng.tensor_scalar(wqa_bf[:, 64 * kt:64 * kt + 64],
                                          wst[:, 64 * kt:64 * kt + 64],
                                          gq_sb[:, kt:kt + 1], None, OP.mult)
                        eng.tensor_scalar(
                            wkc_bf[:, 128 * kt:128 * kt + 128],
                            wst[:, 1024 + 128 * kt:1024 + 128 * kt + 128],
                            gkv_sb[:, kt:kt + 1], None, OP.mult)
                    eng.tensor_single_scalar(
                        wqb_bf[0:64, :], wst[0:64, 3072:3584],
                        RSD0, OP.mult)
                    return eng.tensor_copy(wkvb_bf[:], wst[:, 3584:4096])
                s.add("vector", fold_w,
                      deps=["ld.gq", "ld.gkv", "ld.wqa", "ld.wkc", "ld.wqb",
                            "ld.wkvb"], event="w.fold1")

                for wh in range(2):
                    ld(f"ld.wv{wh}",
                       wst[:].rearrange("p (kt n) -> p kt n", n=256),
                       D["wkva_v"][:, 256 * wh:256 * wh + 256].rearrange(
                           "(kt p) n -> p kt n", p=P),
                       deps=["w.fold1"] if wh == 0 else ["w.fold2_0"])

                    def fold_wv(wh=wh):
                        def f(eng):
                            last = None
                            for kt in range(KT):
                                last = eng.tensor_scalar(
                                    wv_bf[:, 512 * kt + 256 * wh:
                                          512 * kt + 256 * wh + 256],
                                    wst[:, 256 * kt:256 * kt + 256],
                                    gkv_sb[:, kt:kt + 1], None, OP.mult)
                            return last
                        return f
                    s.add("vector", fold_wv(), deps=[f"ld.wv{wh}"],
                          event=f"w.fold2_{wh}")
                s.add("vector", lambda eng: eng.engine_nop(),
                      deps=["w.fold2_1"], event="w.fold2")

                r1, n1 = emit_ln(s, "ln1", TOK,
                                 lambda kt: x_sb[:, TOK * kt:TOK * kt + TOK],
                                 True, sqb, rows, ps_s, ps_q,
                                 ones_cf[:], ones_cb[:], eps_c[0:1, 0:1],
                                 deps=["ld.x", "consts"])
                emit_ln_apply(s, "ln1", TOK,
                              lambda kt: x_sb[:, TOK * kt:TOK * kt + TOK],
                              lambda kt: xn[:, TOK * kt:TOK * kt + TOK],
                              r1, n1, ps_bc, ones_rf[0:1, :], asc, deps=[])
                r2, n2 = emit_ln(s, "ln2", TOK,
                                 lambda kt: xn[:, TOK * kt:TOK * kt + TOK],
                                 False, sqb, rows, ps_s, ps_q,
                                 ones_cf[:], ones_cb[:], eps_c[0:1, 0:1],
                                 deps=["ln1.done"])
                emit_ln_apply(s, "ln2", TOK,
                              lambda kt: xn[:, TOK * kt:TOK * kt + TOK],
                              lambda kt: yh[:, TOK * kt:TOK * kt + TOK],
                              r2, n2, ps_bc, ones_rf[0:1, :], asc,
                              deps=["ln1.done"])

                s.add("sync", lambda eng: eng.dma_start(
                    out=agi.ap().rearrange("(kt p) t -> p kt t", p=P),
                    in_=yh[:].rearrange("p (kt t) -> p kt t", t=TOK)),
                    deps=["ln2.done"], event="agi.done", dma=True)
                s.add("gpsimd", lambda eng: eng.collective_compute(
                    "AllGather", OP.bypass, replica_groups=GROUPS,
                    ins=[agi.ap().opt()], outs=[ago.ap().opt()]),
                    deps=["agi.done"], event="ag1", coll=True)
                if PH == 1:
                    s.add("gpsimd", lambda eng: eng.dma_start(
                        out=xo.rearrange("(kt p) t -> p kt t", p=P),
                        in_=yh[:].rearrange("p (kt t) -> p kt t", t=TOK)),
                        deps=["ln2.done", "ag1"], event="dbg1", dma=True)
                s.barrier()

        if PH >= 2:
            with ExitStack() as mid:
                sbm = lambda n_, sh, dt: mid.enter_context(
                    nc.sbuf_tensor(n_, sh, dt))
                cs_t = sbm("cs_t", [P, T4], BF16)
                sn_t = sbm("sn_t", [P, T4], BF16)
                q_bf = sbm("q_bf", [P, 4 * T4], BF16)
                k_bf = sbm("k_bf", [P, 4 * T4], BF16)
                v_bf = sbm("v_bf", [P, 32 * 512], BF16)

                s.add("vector", lambda eng: (
                    eng.memset(cs_t[64:128, :], 1.0),
                    eng.memset(sn_t[64:128, :], 0.0))[-1], event="consts2")

                with ExitStack() as ph2:
                    sb2 = lambda n_, sh, dt: ph2.enter_context(
                        nc.sbuf_tensor(n_, sh, dt))
                    yhg = sb2("yhg", [P, KT * 512], BF16)
                    qa_bf = sb2("qa_bf", [64, T4], BF16)
                    kc_bf = sb2("kc_bf", [P, T4], BF16)
                    rope_scope = ExitStack()
                    sbr = lambda n_, sh, dt: rope_scope.enter_context(
                        nc.sbuf_tensor(n_, sh, dt))
                    thf = sbr("thf", [64, 1024], F32)
                    thc = sbr("thc", [64, 1024], F32)
                    thi = sbr("thi", [64, 1024], I32)
                    sintmp = sbr("sintmp", [64, 1024], F32)
                    fv = sbr("fv", [64, 1], F32)
                    fv_row = sbr("fv_row", [1, 64], F32)
                    pos_sb = sbr("pos_sb", [1, 1024], F32)
                    jv_sb = sbr("jv_sb", [64, 1], F32)
                    with ExitStack() as ph2p:
                        ps2 = lambda n_, sh: ph2p.enter_context(
                            nc.psum_tensor(n_, sh, F32))
                        ps_th = ps2("ps_th", [64, 1024])
                        ps_qa = ps2("ps_qa", [64, 512])
                        ps_kc = ps2("ps_kc", [P, 512])
                        ps_v = ps2("ps_v", [P, 512])
                        ps_pr = ps2("ps_pr", [P, 1024])

                        ld("ld.jv", jv_sb[:], D["jv"])

                        s.add("scalar", lambda eng: eng.activation(
                            fv[:], jv_sb[:], AF.Exp, bias=0.0,
                            scale=-math.log(10000.0) / 32.0),
                            deps=["ld.jv"], event="rp.fv")
                        s.add("tensor", lambda eng: eng.transpose(
                            ps_th[0:1, 0:64], fv[:, 0:1], identf[0:64, 0:64]),
                            deps=["rp.fv", "ld.identf"], event="rp.fvT")
                        s.add("vector", lambda eng: eng.tensor_copy(
                            fv_row[:], ps_th[0:1, 0:64]),
                            deps=["rp.fvT"], event="rp.fvrow")

                        CW1, CW2 = 6.28125, 0.0019353071795864768
                        CW3 = 2 * math.pi - CW1 - CW2

                        for ch in range(4):
                            def mk_thmm(ch=ch):
                                def f(eng):
                                    last = None
                                    for c2 in range(2):
                                        last = eng.matmul(
                                            ps_th[0:64, 512 * c2:512 * c2 + 512],
                                            fv_row[0:1, 0:64],
                                            pos_sb[0:1, 512 * c2:512 * c2 + 512],
                                            start=True, stop=True)
                                    return last
                                return f
                            ld(f"ld.pos{ch}", pos_sb[:],
                               D["posv"][0:1, 1024 * ch:1024 * ch + 1024],
                               deps=[f"rp.red{ch-1}b"] if ch else [])
                            dd = ["rp.fvrow", f"ld.pos{ch}"]
                            if ch:
                                dd.append(f"rp.red{ch-1}b")
                            s.add("tensor", mk_thmm(), deps=dd, event=f"rp.th{ch}")

                            def mk_red(add_hp):
                                def f(eng):
                                    eng.tensor_single_scalar(
                                        thf[:, :], ps_th[0:64, :], 1.0, OP.mult)
                                    if add_hp:
                                        eng.tensor_single_scalar(
                                            thf[:, :], thf[:, :], math.pi / 2,
                                            OP.add)
                                    eng.tensor_single_scalar(
                                        thc[:, :], thf[:, :],
                                        1.0 / (2 * math.pi), OP.mult)
                                    eng.tensor_copy(thi[:, :], thc[:, :])
                                    eng.tensor_copy(thc[:, :], thi[:, :])
                                    eng.scalar_tensor_tensor(
                                        thf[:, :], thc[:, :], -CW1, thf[:, :],
                                        OP.mult, OP.add)
                                    eng.scalar_tensor_tensor(
                                        thf[:, :], thc[:, :], -CW2, thf[:, :],
                                        OP.mult, OP.add)
                                    return eng.scalar_tensor_tensor(
                                        thf[:, :], thc[:, :], -CW3, thf[:, :],
                                        OP.mult, OP.add)
                                return f
                            s.add("vector", mk_red(False),
                                  deps=[f"rp.th{ch}"], event=f"rp.red{ch}a")
                            s.add("scalar",
                                  (lambda ch=ch: (lambda eng: eng.activation(
                                      sintmp[:, :], thf[0:64, :], AF.Sin)))(),
                                  deps=[f"rp.red{ch}a"], event=f"rp.sin{ch}")
                            s.add("vector", mk_red(True),
                                  deps=[f"rp.sin{ch}", f"rp.th{ch}"],
                                  event=f"rp.red{ch}b")
                            s.add("scalar",
                                  (lambda ch=ch: (lambda eng: eng.activation(
                                      thc[0:64, :], thf[0:64, :], AF.Sin)))(),
                                  deps=[f"rp.red{ch}b"], event=f"rp.cos{ch}")

                            def mk_place(ch=ch):
                                def f(eng):
                                    sl = slice(1024 * ch, 1024 * ch + 1024)
                                    eng.tensor_single_scalar(
                                        sn_t[0:32, sl], sintmp[0:32, :],
                                        -1.0, OP.mult)
                                    eng.tensor_copy(sn_t[32:64, sl],
                                                    sintmp[32:64, :])
                                    return eng.tensor_copy(cs_t[0:64, sl],
                                                           thc[0:64, :])
                                return f
                            s.add("vector", mk_place(),
                                  deps=[f"rp.cos{ch}", "consts2"],
                                  event=f"rp.pl{ch}")

                        s.add("vector", lambda eng: eng.engine_nop(),
                              deps=[f"rp.pl{ch}" for ch in range(4)],
                              event="rp.done")

                        if P2L == 0:
                            s.add("gpsimd", lambda eng: eng.dma_start(
                                out=xo[0:512, :].rearrange("(a p) t -> p a t", p=P),
                                in_=cs_t[:].rearrange("p (a t) -> p a t", t=TOK)),
                                deps=["rp.done"], event="dbgcs", dma=True)
                            s.add("gpsimd", lambda eng: eng.dma_start(
                                out=xo[512:1024, :].rearrange("(a p) t -> p a t", p=P),
                                in_=sn_t[:].rearrange("p (a t) -> p a t", t=TOK)),
                                deps=["rp.done"], event="dbgsn", dma=True)
                        for hb in range(8 if P2L >= 1 else 0):
                            tb, th = hb // 2, hb % 2
                            ld(f"p2.y{hb}",
                               yhg[:].rearrange("p (kt t) -> p kt t", t=512),
                               ago.ap()[E * tb:E * (tb + 1),
                                        512 * th:512 * th + 512].rearrange(
                                   "(kt p) t -> p kt t", p=P),
                               deps=["ag1"] + ([f"p2.hbdone{hb-1}"] if hb else []))

                            def mk1(hb=hb):
                                def f(eng):
                                    last = None
                                    for kt in range(KT):
                                        rhs = yhg[:, 512 * kt:512 * kt + 512]
                                        eng.matmul(ps_qa[0:64, :],
                                                   wqa_bf[:, 64 * kt:64 * kt + 64],
                                                   rhs, start=(kt == 0),
                                                   stop=(kt == KT - 1))
                                        last = eng.matmul(
                                            ps_kc[:, :],
                                            wkc_bf[:, 128 * kt:128 * kt + 128],
                                            rhs, start=(kt == 0),
                                            stop=(kt == KT - 1))
                                    return last
                                return f
                            dd = [f"p2.y{hb}", "w.fold1"]
                            if hb:
                                dd.append(f"p2.ev{hb-1}")
                            s.add("tensor", mk1(), deps=dd, event=f"p2.qakc{hb}")

                            def mk_ev(hb=hb):
                                def f(eng):
                                    c0 = 512 * hb
                                    eng.tensor_copy(qa_bf[0:64, c0:c0 + 512],
                                                    ps_qa[0:64, :])
                                    return eng.tensor_copy(kc_bf[:, c0:c0 + 512],
                                                           ps_kc[:, :])
                                return f
                            s.add("vector", mk_ev(), deps=[f"p2.qakc{hb}"],
                                  event=f"p2.ev{hb}")

                            for tt in range(4):
                                i = hb * 4 + tt

                                def mk_v(hb=hb, tt=tt):
                                    def f(eng):
                                        last = None
                                        for kt in range(KT):
                                            lhs = yhg[:, 512 * kt + 128 * tt:
                                                      512 * kt + 128 * tt + 128]
                                            last = eng.matmul(
                                                ps_v[:, :], lhs,
                                                wv_bf[:, 512 * kt:512 * kt + 512],
                                                start=(kt == 0),
                                                stop=(kt == KT - 1))
                                        return last
                                    return f
                                dd = [f"p2.y{hb}", "w.fold2"]
                                if i:
                                    dd.append(f"p2.vev{i-1}")
                                s.add("tensor", mk_v(), deps=dd,
                                      event=f"p2.vmm{i}")
                                s.add("vector",
                                      (lambda i=i: (lambda eng: eng.tensor_copy(
                                          v_bf[:, 512 * i:512 * i + 512],
                                          ps_v[:, :])))(),
                                      deps=[f"p2.vmm{i}"], event=f"p2.vev{i}")

                            s.add("vector", lambda eng: eng.engine_nop(),
                                  deps=[f"p2.ev{hb}", f"p2.vev{hb * 4 + 3}"],
                                  event=f"p2.hbdone{hb}")

                        rope_scope.close()
                        qpre = sb2("qpre", [P, T4], BF16)
                        rot = sb2("rot", [64, T4], BF16)
                        t2s = sb2("t2s", [64, 2048], F32)
                        if P2L == 1:
                            s.add("gpsimd", lambda eng: eng.dma_start(
                                out=xo[0:512, :].rearrange("(a p) t -> p a t", p=P),
                                in_=kc_bf[0:P, 0:T4].rearrange("p (a t) -> p a t", t=TOK)),
                                deps=["p2.hbdone7"], event="dbgkc", dma=True)
                        nht_count = [0]
                        for which, src_, wt, dst, kdim in ((
                                ("q", qa_bf, wqb_bf, q_bf, 64),
                                ("k", kc_bf, wkvb_bf, k_bf, 128))
                                if P2L >= 2 else ()):
                            for ht in range(4):
                                nht_count[0] += 1
                                if nht_count[0] > NHT:
                                    continue
                                tagh = f"p2.{which}{ht}"
                                if which == "q":
                                    prev_rope = f"p2.q{ht-1}.rope" if ht else None
                                else:
                                    prev_rope = (f"p2.k{ht-1}.rope" if ht
                                                 else "p2.q3.rope")
                                for c8 in range(8):
                                    def mk_mm(src_=src_, wt=wt, ht=ht, kdim=kdim,
                                              c8=c8):
                                        def f(eng):
                                            return eng.matmul(
                                                ps_pr[:, 512 * (c8 % 2):
                                                      512 * (c8 % 2) + 512],
                                                wt[0:kdim,
                                                   128 * ht:128 * ht + 128],
                                                src_[0:kdim,
                                                     512 * c8:512 * c8 + 512],
                                                start=True, stop=True)
                                        return f
                                    dd = ["p2.hbdone7", "w.fold1"]
                                    if c8 >= 2:
                                        dd.append(f"{tagh}.e{c8-2}")
                                    elif prev_rope:
                                        ptag = prev_rope.rsplit(".", 1)[0]
                                        dd.append(f"{ptag}.e{c8 + 6}")
                                    s.add("tensor", mk_mm(), deps=dd,
                                          event=f"{tagh}.m{c8}")
                                    dde_ = [f"{tagh}.m{c8}", "rp.done"]
                                    if prev_rope:
                                        dde_.append(prev_rope)
                                    s.add("vector",
                                          (lambda c8=c8: (lambda eng:
                                           eng.tensor_copy(
                                               qpre[:, 512 * c8:512 * c8 + 512],
                                               ps_pr[:, 512 * (c8 % 2):
                                                     512 * (c8 % 2) + 512])))(),
                                          deps=dde_,
                                          event=f"{tagh}.e{c8}")
                                alldone = [f"{tagh}.e{c}" for c in range(8)]
                                if P2L >= 3:
                                    s.add("sync", lambda eng: eng.dma_start(
                                        out=rot[0:32, :], in_=qpre[32:64, :]),
                                        deps=alldone, event=f"{tagh}.r1", dma=True)
                                    s.add("sync", lambda eng: eng.dma_start(
                                        out=rot[32:64, :], in_=qpre[0:32, :]),
                                        deps=alldone, event=f"{tagh}.r2", dma=True)

                                def mk_rope(ht=ht, dst=dst):
                                    def f(eng):
                                        last = None
                                        for c2 in range(2):
                                            sl = slice(2048 * c2, 2048 * c2 + 2048)
                                            eng.tensor_tensor(
                                                t2s[0:64, :], qpre[0:64, sl],
                                                cs_t[0:64, sl], OP.mult)
                                            eng.tensor_tensor(
                                                rot[0:64, sl], rot[0:64, sl],
                                                sn_t[0:64, sl], OP.mult)
                                            eng.tensor_tensor(
                                                dst[0:64,
                                                    T4 * ht + 2048 * c2:
                                                    T4 * ht + 2048 * c2 + 2048],
                                                t2s[0:64, :], rot[0:64, sl],
                                                OP.add)
                                            last = eng.tensor_copy(
                                                dst[64:128,
                                                    T4 * ht + 2048 * c2:
                                                    T4 * ht + 2048 * c2 + 2048],
                                                qpre[64:128, sl])
                                        return last
                                    return f
                                if P2L >= 4:
                                    s.add("vector", mk_rope(),
                                          deps=[f"{tagh}.r1", f"{tagh}.r2",
                                                "rp.done"],
                                          event=f"{tagh}.rope")
                                else:
                                    rdeps = list(alldone) + ["rp.done"]
                                    if P2L >= 3:
                                        rdeps += [f"{tagh}.r1", f"{tagh}.r2"]

                                    def mk_cp(dst=dst, ht=ht):
                                        def f(eng):
                                            return eng.tensor_copy(
                                                dst[:, T4 * ht:T4 * ht + T4],
                                                qpre[:, :])
                                        return f
                                    s.add("vector", mk_cp(), deps=rdeps,
                                          event=f"{tagh}.rope")

                        pdone_deps = ["rp.done"]
                        if P2L >= 1:
                            pdone_deps.append("p2.hbdone7")
                        if P2L >= 2:
                            allr = (["p2.q%d.rope" % i for i in range(4)] +
                                    ["p2.k%d.rope" % i for i in range(4)])
                            pdone_deps += allr[:NHT]
                        s.add("vector", lambda eng: eng.engine_nop(),
                              deps=pdone_deps, event="p2.done")
                        if PH == 2 and not NODUMP:
                            dumpw = (q_bf if os.environ.get("L0DBG", "q")
                                     == "q" else k_bf)
                            s.add("gpsimd",
                                  (lambda dw=dumpw: (lambda eng:
                                   eng.dma_start(
                                       out=xo.rearrange(
                                           "(kt p) t -> p kt t", p=P),
                                       in_=dw[:].rearrange(
                                           "p (kt t) -> p kt t",
                                           t=TOK))))(),
                                  deps=["p2.done"], event="dbg2", dma=True)
                        s.barrier()

                if PH >= 3:
                  with ExitStack() as ph3:
                      sb3 = lambda n_, sh, dt: ph3.enter_context(
                          nc.sbuf_tensor(n_, sh, dt))
                      p_bf = sb3("p_bf", [P, T4], BF16)
                      pt_bf = sb3("pt_bf", [P, 32 * 512], BF16)
                      sums = sb3("sums", [P, 8], F32)
                      nmax = sb3("nmax", [P, 4], F32)
                      srow = sb3("srow", [1, 512], F32)
                      rrow = sb3("rrow", [1, 512], F32)
                      rec_sb = sb3("rec_sb", [P, 512], F32)
                      yo_bf = sb3("yo_bf", [P, 512], BF16)
                      with ExitStack() as ph3p:
                          ps_sc = ph3p.enter_context(
                              nc.psum_tensor("ps_sc", [P, 8192], BF16))
                          sc32 = ps_sc[:].bitcast(F32)
                          ptv = pt_bf[:].rearrange("p (kb q) -> p kb q", q=512)

                          prev_pv = None
                          for h in range(4):
                              for stv in range(8):
                                  nkb = 4 * (stv + 1)
                                  for qi in range(4):
                                      qg = 4 * stv + qi
                                      kmax = 128 * (qg + 1)
                                      nch = (kmax + 511) // 512
                                      tag = f"a{h}_{qg}"

                                      def mk_qk(h=h, qg=qg, kmax=kmax, nch=nch):
                                          def f(eng):
                                              last = None
                                              for c in range(nch):
                                                  n = min(512, kmax - 512 * c)
                                                  last = eng.matmul(
                                                      sc32[:, 512 * c:512 * c + n],
                                                      q_bf[:, T4 * h + 128 * qg:
                                                           T4 * h + 128 * qg + 128],
                                                      k_bf[:, T4 * h + 512 * c:
                                                           T4 * h + 512 * c + n],
                                                      start=True, stop=True)
                                              return last
                                          return f
                                      dd = ["p2.done", "ld.ctile"]
                                      if qg:
                                          dd.append(f"a{h}_{qg-1}.pev")
                                      elif h:
                                          dd.append(f"a{h-1}_31.pev")
                                      if prev_pv:
                                          dd.append(prev_pv)
                                      s.add("tensor", mk_qk(), deps=dd,
                                            event=f"{tag}.qk")

                                      def mk_mx(kmax=kmax, qi=qi):
                                          def f(eng):
                                              eng.tensor_tensor(
                                                  sc32[:, kmax - 128:kmax],
                                                  sc32[:, kmax - 128:kmax],
                                                  ctile[:], OP.add)
                                              return eng.reduce_max(
                                                  nmax[:, qi:qi + 1],
                                                  sc32[:, 0:kmax],
                                                  axis=AX.X, negate=True)
                                          return f
                                      s.add("vector", mk_mx(), deps=[f"{tag}.qk"],
                                            event=f"{tag}.mx")

                                      def mk_exp(kmax=kmax, qi=qi, stv=stv):
                                          def f(eng):
                                              so = 4 * (stv % 2) + qi
                                              return eng.activation(
                                                  p_bf[:, 0:kmax], sc32[:, 0:kmax],
                                                  AF.Exp, bias=nmax[:, qi:qi + 1],
                                                  scale=1.0,
                                                  accum_out=sums[:, so:so + 1])
                                          return f
                                      dde = [f"{tag}.mx"]
                                      Lst = 8 * h + stv
                                      if Lst >= 2:
                                          hp, sp_ = (Lst - 2) // 8, (Lst - 2) % 8
                                          dde.append(f"b{hp}_{sp_}.str")
                                      s.add("scalar", mk_exp(), deps=dde,
                                            event=f"{tag}.exp")

                                      nkbq = kmax // 128
                                      ng = (nkbq + 3) // 4
                                      for g in range(ng):
                                          nn = min(4 * g + 4, nkbq) - 4 * g

                                          def mk_tr(g=g, nn=nn):
                                              def f(eng):
                                                  last = None
                                                  for j in range(nn):
                                                      sl = (1024 * (g % 2) + 512 +
                                                            128 * j)
                                                      last = eng.transpose(
                                                          ps_sc[:, sl:sl + 128],
                                                          p_bf[:,
                                                               128 * (4 * g + j):
                                                               128 * (4 * g + j) +
                                                               128],
                                                          ident[:])
                                                  return last
                                              return f
                                          ddt = [f"{tag}.exp", "ld.ident"]
                                          if g >= 2:
                                              ddt.append(f"{tag}.te{g-2}")
                                          s.add("tensor", mk_tr(), deps=ddt,
                                                event=f"{tag}.tr{g}")

                                          def mk_te(qi=qi, g=g, nn=nn):
                                              def f(eng):
                                                  sl = 1024 * (g % 2) + 512
                                                  return eng.tensor_copy(
                                                      ptv[:, 4 * g:4 * g + nn,
                                                          128 * qi:128 * qi + 128],
                                                      ps_sc[:, sl:sl + 128 * nn]
                                                      .rearrange(
                                                          "p (a b) -> p a b", b=128))
                                              return f
                                          s.add("vector", mk_te(),
                                                deps=[f"{tag}.tr{g}"],
                                                event=f"{tag}.te{g}")
                                      s.add("vector", lambda eng: eng.engine_nop(),
                                            deps=[f"{tag}.te{g}"
                                                  for g in range(ng)],
                                            event=f"{tag}.pev")

                                  tps = f"b{h}_{stv}"
                                  qdone = [f"a{h}_{4 * stv + qi}.pev"
                                           for qi in range(4)]

                                  def mk_pv(h=h, stv=stv, nkb=nkb):
                                      def f(eng):
                                          last = None
                                          for kb in range(nkb):
                                              j = kb - 4 * stv
                                              lo = 128 * j if j > 0 else 0
                                              last = eng.matmul(
                                                  sc32[:, lo:512],
                                                  v_bf[:, 512 * kb + 128 * h:
                                                       512 * kb + 128 * h + 128],
                                                  ptv[:, kb, lo:512],
                                                  start=(kb == 0),
                                                  stop=(kb == nkb - 1))
                                          return last
                                      return f
                                  s.add("tensor", mk_pv(), deps=qdone,
                                        event=f"{tps}.pv")

                                  def mk_strp(stv=stv):
                                      def f(eng):
                                          last = None
                                          for qi in range(4):
                                              so = 4 * (stv % 2) + qi
                                              last = eng.transpose(
                                                  sc32[0:1, 512 + 128 * qi:
                                                       512 + 128 * qi + 128],
                                                  sums[:, so:so + 1], identf[:])
                                          return last
                                      return f
                                  s.add("tensor", mk_strp(),
                                        deps=qdone + [f"{tps}.pv"],
                                        event=f"{tps}.str")
                                  s.add("vector",
                                        (lambda: (lambda eng: eng.tensor_copy(
                                            srow[0:1, :], sc32[0:1, 512:1024])))(),
                                        deps=[f"{tps}.str"], event=f"{tps}.srow")
                                  s.add("vector",
                                        (lambda: (lambda eng: eng.reciprocal(
                                            rrow[0:1, :], srow[0:1, :])))(),
                                        deps=[f"{tps}.srow"], event=f"{tps}.rrow")
                                  s.add("tensor",
                                        (lambda: (lambda eng: eng.matmul(
                                            sc32[:, 512:1024], ones_rf[0:1, :],
                                            rrow[0:1, :], start=True,
                                            stop=True)))(),
                                        deps=[f"{tps}.rrow", f"{tps}.srow"],
                                        event=f"{tps}.bc")
                                  s.add("vector",
                                        (lambda: (lambda eng: eng.tensor_copy(
                                            rec_sb[:, :], sc32[:, 512:1024])))(),
                                        deps=[f"{tps}.bc"], event=f"{tps}.rec")
                                  ddyo = [f"{tps}.rec", f"{tps}.pv"]
                                  Lst = 8 * h + stv
                                  if Lst >= 1:
                                      hp, sp_ = (Lst - 1) // 8, (Lst - 1) % 8
                                      ddyo.append(f"b{hp}_{sp_}.yw")
                                  s.add("vector",
                                        (lambda: (lambda eng:
                                         eng.scalar_tensor_tensor(
                                             yo_bf[:, :], sc32[:, 0:512], 0.0,
                                             rec_sb[:, :], OP.bypass,
                                             OP.mult)))(),
                                        deps=ddyo, event=f"{tps}.yo")
                                  s.add("sync",
                                        (lambda h=h, stv=stv:
                                         (lambda eng: eng.dma_start(
                                             out=ygi.ap()[128 * h:128 * h + 128,
                                                          512 * stv:
                                                          512 * stv + 512],
                                             in_=yo_bf[:, :])))(),
                                        deps=[f"{tps}.yo"], event=f"{tps}.yw",
                                        dma=True)
                                  prev_pv = f"{tps}.yo"

                          s.barrier()

            if PH == 3:
                for a_ in range(4):
                    s.add("gpsimd",
                          (lambda a_=a_: (lambda eng: eng.dma_start(
                              out=xo[512 * a_:512 * a_ + 512, :].rearrange(
                                  "(b p) t -> p b t", p=P),
                              in_=ygi.ap()[128 * a_:128 * a_ + 128, :]
                              .rearrange("p (b t) -> p b t", t=TOK))))(),
                          deps=[f"b{a_}_{stv}.yw" for stv in range(8)],
                          event=f"dbg3_{a_}", dma=True)
            if PH >= 4:
              s.add("gpsimd", lambda eng: eng.collective_compute(
                  "AllGather", OP.bypass, replica_groups=GROUPS,
                  ins=[ygi.ap().opt()], outs=[ygo.ap().opt()]),
                  deps=[f"b{h}_{stv}.yw" for h in range(4) for stv in range(8)],
                  event="ag2", coll=True)

              with ExitStack() as ph4:
                  sb4 = lambda n_, sh, dt: ph4.enter_context(
                      nc.sbuf_tensor(n_, sh, dt))
                  ya = sb4("ya", [P, KT * TOK], BF16)
                  x2 = sb4("x2", [P, KT * TOK], F32)
                  wof = sb4("wof", [P, KT * 256], F32)
                  wob = sb4("wob", [P, KT * 256], BF16)
                  with ExitStack() as ph4p:
                      ps_o = [ph4p.enter_context(
                          nc.psum_tensor(f"ps_o{i}", [P, 512], F32))
                          for i in range(4)]

                      def ya_load(eng):
                          for arm in nc.switch_core_id(engines=[eng]):
                              u = arm.logical % 4
                              eng.dma_start(
                                  out=ya[:].rearrange("p (kt t) -> p kt t", t=TOK),
                                  in_=ygo.ap()[:, TOK * u:TOK * u + TOK].rearrange(
                                      "(kt p) t -> p kt t", p=P)).then_inc(
                                  s.sems["sync"], 16)
                          return None
                      s.add("sync", ya_load, deps=["ag2"], self_inc=16)
                      s.events["p4.ya"] = ("sync", s.counts["sync"])

                      ld("p4.x2", x2[:].rearrange("p (kt t) -> p kt t", t=TOK),
                         D["x"].rearrange("(kt p) t -> p kt t", p=P))

                      for etg in range(8):
                          ld(f"p4.wo{etg}",
                             wof[:].rearrange("p (kt n) -> p kt n", n=256),
                             D["wo"][:, 256 * etg:256 * etg + 256].rearrange(
                                 "(kt p) n -> p kt n", p=P),
                             deps=[f"p4.cast{etg-1}"] if etg else [])
                          ddc = [f"p4.wo{etg}"]
                          if etg:
                              ddc += [f"p4.om{2 * (etg - 1) + e2}_{t2}"
                                      for e2 in range(2) for t2 in range(2)]
                          s.add("vector",
                                (lambda: (lambda eng: eng.tensor_copy(
                                    wob[:], wof[:])))(),
                                deps=ddc, event=f"p4.cast{etg}")

                          for et2 in range(2):
                              et = 2 * etg + et2
                              for th in range(2):
                                  pi = 2 * et2 + th

                                  def mk_om(et2=et2, th=th, pi=pi):
                                      def f(eng):
                                          last = None
                                          for jc in range(KT):
                                              last = eng.matmul(
                                                  ps_o[pi][:, :],
                                                  wob[:, 256 * jc + 128 * et2:
                                                      256 * jc + 128 * et2 + 128],
                                                  ya[:, TOK * jc + 512 * th:
                                                     TOK * jc + 512 * th + 512],
                                                  start=(jc == 0),
                                                  stop=(jc == KT - 1))
                                          return last
                                      return f
                                  dd = [f"p4.cast{etg}", "p4.ya"]
                                  if etg:
                                      dd.append(f"p4.oev{2 * (etg - 1) + et2}_{th}")
                                  s.add("tensor", mk_om(), deps=dd,
                                        event=f"p4.om{et}_{th}")

                                  def mk_oev(et=et, th=th, pi=pi):
                                      def f(eng):
                                          xsl = x2[:, TOK * et + 512 * th:
                                                   TOK * et + 512 * th + 512]
                                          return eng.scalar_tensor_tensor(
                                              xsl, ps_o[pi][:, :], 0.0, xsl,
                                              OP.bypass, OP.add)
                                      return f
                                  s.add("vector", mk_oev(),
                                        deps=[f"p4.om{et}_{th}", "p4.x2"],
                                        event=f"p4.oev{et}_{th}")

                      s.add("sync", lambda eng: eng.dma_start(
                          out=xo.rearrange("(kt p) t -> p kt t", p=P),
                          in_=x2[:].rearrange("p (kt t) -> p kt t", t=TOK)),
                          deps=[f"p4.oev{et}_{th}" for et in range(16)
                                for th in range(2)],
                          event="p4.xout", dma=True)


        s.fence_all()
        blk = top.enter_context(nc.Block("main"))
        s.emit(blk)
    return nc





def prep_l0ffn(inputs):
    gp = _u8view(np.asarray(inputs["L0_ffn_gate_W_packed"]).T)
    up = _u8view(np.asarray(inputs["L0_ffn_up_W_packed"]).T)
    dp = _u8view(np.asarray(inputs["L0_ffn_down_W_packed"]).T)
    gs = _fm(np.asarray(inputs["L0_ffn_gate_scales"])).astype(np.float32)
    us = _fm(np.asarray(inputs["L0_ffn_up_scales"])).astype(np.float32)
    ds = _fm(np.asarray(inputs["L0_ffn_down_scales"])).astype(np.float32)
    spv = np.zeros((P, 1), np.int16)
    spv[1::2, 0] = 4
    i16c = np.broadcast_to(np.array([7, 8, 12, 6, 15, 4, 0, 0], np.int16),
                           (P, 8)).copy()
    common = {"gp": gp, "up_": up, "dp": dp, "gs": gs, "us": us, "ds": ds,
              "spv": spv, "i16c": i16c}
    return common


def build_l0ffn():
    nc = bass.Bass("TRN2", target_bir_lowering=False)
    D = {}
    for name, shape, dt in [
        ("x", [E, TOK], F32),
        ("gp", [1024, 16384], U8), ("up_", [1024, 16384], U8),
        ("dp", [8192, 2048], U8),
        ("gs", [64, 16384], F32), ("us", [64, 16384], F32),
        ("ds", [512, 2048], F32),
        ("spv", [P, 1], I16), ("i16c", [P, 8], I16),
    ]:
        D[name] = nc.dram_tensor(name, shape, dt, kind="ExternalInput").ap()
    xo = nc.dram_tensor("xo", [E, TOK], F32, kind="ExternalOutput").ap()
    inter_d = nc.dram_tensor("inter_d", [16384, TOK], BF16)

    with ExitStack() as top:
        s = Sched(nc, top)
        sb = lambda n_, sh, dt: top.enter_context(nc.sbuf_tensor(n_, sh, dt))
        ones_cb = sb("ones_cb", [P, 1], BF16)
        ones_cf = sb("ones_cf", [P, 1], F32)
        ones_rf = sb("ones_rf", [1, P], F32)
        eps_c = sb("eps_c", [P, 1], F32)
        sp_c = sb("sp_c", [P, 1], I16)
        i16_c = sb("i16_c", [P, 8], I16)
        xn = sb("xn", [P, KT * TOK], BF16)

        s.add("vector", lambda eng: (
            eng.memset(ones_cb[:], 1.0), eng.memset(ones_cf[:], 1.0),
            eng.memset(ones_rf[:], 1.0), eng.memset(eps_c[:], EPS),
        )[-1], event="consts")

        def ld(name, out_ap, in_ap, deps=(), eng_name="sync"):
            s.add(eng_name, (lambda o=out_ap, i=in_ap:
                             (lambda eng: eng.dma_start(out=o, in_=i)))(),
                  deps=deps, event=name, dma=True)

        ld("ld.sp", sp_c[:], D["spv"])
        ld("ld.i16", i16_c[:], D["i16c"])
        c7 = i16_c[:, 0:1]
        c8_ = i16_c[:, 1:2]
        c12 = i16_c[:, 2:3]
        c6 = i16_c[:, 3:4]

        with ExitStack() as ph:
            sbp = lambda n_, sh, dt: ph.enter_context(nc.sbuf_tensor(n_, sh, dt))
            x_sb = sbp("x_sb", [P, KT * TOK], F32)
            sqb = sbp("sqb", [P, 2 * TOK], F32)
            rows = sbp("rows", [1, 6 * TOK], F32)
            asc = sbp("asc", [P, 512], F32)
            with ExitStack() as php:
                psp = lambda n_, sh: php.enter_context(
                    nc.psum_tensor(n_, sh, F32))
                ps_s = psp("ps_s", [1, TOK])
                ps_q = psp("ps_q", [1, TOK])
                ps_bc = psp("ps_bc", [P, 1024])
                ld("ld.x", x_sb[:].rearrange("p (kt t) -> p kt t", t=TOK),
                   D["x"].rearrange("(kt p) t -> p kt t", p=P))
                r1, n1 = emit_ln(s, "ln", TOK,
                                 lambda kt: x_sb[:, TOK * kt:TOK * kt + TOK],
                                 True, sqb, rows, ps_s, ps_q,
                                 ones_cf[:], ones_cb[:], eps_c[0:1, 0:1],
                                 deps=["ld.x", "consts"])
                emit_ln_apply(s, "ln", TOK,
                              lambda kt: x_sb[:, TOK * kt:TOK * kt + TOK],
                              lambda kt: xn[:, TOK * kt:TOK * kt + TOK],
                              r1, n1, ps_bc, ones_rf[0:1, :], asc, deps=[])
                s.barrier()

        with ExitStack() as ph2:
            sb2 = lambda n_, sh, dt: ph2.enter_context(
                nc.sbuf_tensor(n_, sh, dt))
            pk2 = sb2("pk2", [P, 16 * 512], U8)
            scl = sb2("scl", [P, 16 * 512], BF16)
            wg_bf = sb2("wg_bf", [P, 16 * 512], BF16)
            wu_bf = sb2("wu_bf", [P, 16 * 512], BF16)
            sA = sb2("sA", [P, 4096], I16)
            sB = sb2("sB", [P, 4096], I16)
            sC = sb2("sC", [P, 4096], I16)
            eg1 = sb2("eg1", [P, 512], F32)
            eg2 = sb2("eg2", [P, 512], F32)
            inter_t = sb2("inter_t", [P, 4 * TOK], BF16)
            with ExitStack() as ph2p:
                ps2 = lambda n_, sh: ph2p.enter_context(
                    nc.psum_tensor(n_, sh, F32))
                ps_g = ps2("ps_g", [P, 1024])
                ps_u = ps2("ps_u", [P, 1024])

                def emit_dec(tag, out_bf, deps):
                    def mk(ch):
                        def f(eng):
                            sl = slice(4096 * ch, 4096 * ch + 4096)
                            w = sA[:, :]
                            eng.tensor_copy(w, pk2[:, sl])
                            eng.tensor_scalar(sB[:, :], w, sp_c[:, 0:1], c7,
                                              OP.logical_shift_right,
                                              OP.bitwise_and)
                            eng.tensor_scalar(sC[:, :], w, sp_c[:, 0:1], c8_,
                                              OP.logical_shift_right,
                                              OP.bitwise_and)
                            eng.scalar_tensor_tensor(sA[:, :], sB[:, :], 2,
                                                     sB[:, :], OP.min, OP.add)
                            eng.tensor_scalar(sB[:, :], sB[:, :], 1, 250,
                                              OP.min, OP.mult)
                            eng.tensor_tensor(sA[:, :], sA[:, :], sB[:, :],
                                              OP.add)
                            eng.tensor_single_scalar(sC[:, :], sC[:, :], c12,
                                                     OP.logical_shift_left)
                            eng.scalar_tensor_tensor(sA[:, :], sA[:, :], c6,
                                                     sC[:, :],
                                                     OP.logical_shift_left,
                                                     OP.bitwise_or)
                            return eng.tensor_tensor(
                                out_bf[:, sl], sA[:, :].bitcast(BF16),
                                scl[:, sl], OP.mult)
                        return f
                    for ch in range(2):
                        s.add("vector", mk(ch),
                              deps=deps + ["ld.sp", "ld.i16"],
                              event=f"{tag}.d{ch}")
                    return [f"{tag}.d{ch}" for ch in range(2)]

                prev_mm_done = None
                prev_int_dma = None
                for ng in range(32):
                    nsl = slice(512 * ng, 512 * ng + 512)
                    decg_ev = []
                    for mat, (pkd, scd, wdst) in enumerate(
                            ((D["gp"], D["gs"], wg_bf),
                             (D["up_"], D["us"], wu_bf))):
                        mtag = f"B{ng}m{mat}"
                        pdeps = []
                        if mat == 1:
                            pdeps = decg_ev
                        elif ng:
                            pdeps = [f"B{ng-1}m1.d0", f"B{ng-1}m1.d1"]
                        for par in range(2):
                            ld(f"{mtag}.pk{par}",
                               bass.AP(pk2, par * 16 * 512,
                                       [[2 * 16 * 512, 64], [512, 16],
                                        [1, 512]]),
                               pkd[:, nsl].rearrange("(kc r) n -> r kc n",
                                                     r=64),
                               deps=pdeps)
                        for a in range(4):
                            ld(f"{mtag}.sc{a}",
                               bass.AP(scl, a * 32 * 16 * 512,
                                       [[16 * 512, 32], [512, 16], [1, 512]]),
                               scd[:, nsl].rearrange(
                                   "(kc four) n -> four kc n",
                                   four=4)[a].unsqueeze(0).broadcast_to(
                                   [32, 16, 512]),
                               deps=pdeps, eng_name="gpsimd")
                        ddec = ([f"{mtag}.pk{p}" for p in range(2)] +
                                [f"{mtag}.sc{a}" for a in range(4)])
                        if prev_mm_done:
                            ddec.append(prev_mm_done)
                        ev = emit_dec(mtag, wdst, ddec)
                        if mat == 0:
                            decg_ev = ev

                    for n128 in range(4):
                        for th in range(2):
                            ttag = f"B{ng}t{n128}_{th}"

                            def mk_mm(ng=ng, n128=n128, th=th):
                                def f(eng):
                                    last = None
                                    for kc in range(KT):
                                        rhs = xn[:, TOK * kc + 512 * th:
                                                 TOK * kc + 512 * th + 512]
                                        eng.matmul(
                                            ps_g[:, 512 * th:512 * th + 512],
                                            wg_bf[:, 512 * kc + 128 * n128:
                                                  512 * kc + 128 * n128 + 128],
                                            rhs, start=(kc == 0),
                                            stop=(kc == KT - 1))
                                        last = eng.matmul(
                                            ps_u[:, 512 * th:512 * th + 512],
                                            wu_bf[:, 512 * kc + 128 * n128:
                                                  512 * kc + 128 * n128 + 128],
                                            rhs, start=(kc == 0),
                                            stop=(kc == KT - 1))
                                    return last
                                return f
                            dmm = [f"B{ng}m0.d0", f"B{ng}m0.d1",
                                   f"B{ng}m1.d0", f"B{ng}m1.d1", "ln.done"]
                            if n128 or th:
                                dmm.append(f"B{ng}t{n128}_{th-1}.ep" if th
                                           else f"B{ng}t{n128-1}_1.ep")
                            elif ng:
                                dmm.append(f"B{ng-1}t3_1.ep")
                            s.add("tensor", mk_mm(), deps=dmm,
                                  event=f"{ttag}.mm")

                            def mk_sq(th=th):
                                def f(eng):
                                    return eng.activation(
                                        eg1[:, :],
                                        ps_g[:, 512 * th:512 * th + 512],
                                        AF.Square)
                                return f
                            s.add("scalar", mk_sq(), deps=[f"{ttag}.mm"],
                                  event=f"{ttag}.sq")

                            def mk_arg(th=th):
                                def f(eng):
                                    eng.tensor_scalar(
                                        eg1[:, :], eg1[:, :],
                                        0.07135806791910402, 1.5957691216057308,
                                        OP.mult, OP.add)
                                    return eng.tensor_tensor(
                                        eg2[:, :], eg1[:, :],
                                        ps_g[:, 512 * th:512 * th + 512],
                                        OP.mult)
                                return f
                            s.add("vector", mk_arg(), deps=[f"{ttag}.sq"],
                                  event=f"{ttag}.arg")
                            s.add("scalar",
                                  (lambda: (lambda eng: eng.activation(
                                      eg1[:, :], eg2[:, :], AF.Sigmoid)))(),
                                  deps=[f"{ttag}.arg"], event=f"{ttag}.sg")

                            def mk_int(ng=ng, n128=n128, th=th):
                                def f(eng):
                                    eng.tensor_tensor(
                                        eg2[:, :], eg1[:, :],
                                        ps_g[:, 512 * th:512 * th + 512],
                                        OP.mult)
                                    io = inter_t[:, TOK * n128 + 512 * th:
                                                 TOK * n128 + 512 * th + 512]
                                    return eng.tensor_tensor(
                                        io, eg2[:, :],
                                        ps_u[:, 512 * th:512 * th + 512],
                                        OP.mult)
                                return f
                            dint = [f"{ttag}.sg"]
                            if prev_int_dma:
                                dint.append(prev_int_dma)
                            s.add("vector", mk_int(), deps=dint,
                                  event=f"{ttag}.ep")
                    prev_mm_done = f"B{ng}t3_1.mm"

                    ld(f"B{ng}.int",
                       inter_d.ap()[512 * ng:512 * ng + 512, :].rearrange(
                           "(j p) t -> p j t", p=P),
                       inter_t[:].rearrange("p (j t) -> p j t", t=TOK),
                       deps=[f"B{ng}t{n1}_{t1}.ep" for n1 in range(4)
                             for t1 in range(2)])
                    prev_int_dma = f"B{ng}.int"
                s.barrier()

        with ExitStack() as ph3:
            sb3 = lambda n_, sh, dt: ph3.enter_context(
                nc.sbuf_tensor(n_, sh, dt))
            x2 = sb3("x2", [P, KT * TOK], F32)
            pk3 = sb3("pk3", [P, 16 * 512], U8)
            scl3 = sb3("scl3", [P, 16 * 512], BF16)
            wd_bf = sb3("wd_bf", [P, 16 * 512], BF16)
            dA = sb3("dA", [P, 4096], I16)
            dB = sb3("dB", [P, 4096], I16)
            dC = sb3("dC", [P, 4096], I16)
            int4 = sb3("int4", [P, 4 * TOK], BF16)
            with ExitStack() as ph3p:
                ps_d = ph3p.enter_context(
                    nc.psum_tensor("ps_d", [P, 8 * 512], F32))

                ld("C.x2", x2[:].rearrange("p (kt t) -> p kt t", t=TOK),
                   D["x"].rearrange("(kt p) t -> p kt t", p=P))

                def emit_dec3(tag, deps):
                    def mk(ch):
                        def f(eng):
                            sl = slice(4096 * ch, 4096 * ch + 4096)
                            w = dA[:, :]
                            eng.tensor_copy(w, pk3[:, sl])
                            eng.tensor_scalar(dB[:, :], w, sp_c[:, 0:1], c7,
                                              OP.logical_shift_right,
                                              OP.bitwise_and)
                            eng.tensor_scalar(dC[:, :], w, sp_c[:, 0:1], c8_,
                                              OP.logical_shift_right,
                                              OP.bitwise_and)
                            eng.scalar_tensor_tensor(dA[:, :], dB[:, :], 2,
                                                     dB[:, :], OP.min, OP.add)
                            eng.tensor_scalar(dB[:, :], dB[:, :], 1, 250,
                                              OP.min, OP.mult)
                            eng.tensor_tensor(dA[:, :], dA[:, :], dB[:, :],
                                              OP.add)
                            eng.tensor_single_scalar(dC[:, :], dC[:, :], c12,
                                                     OP.logical_shift_left)
                            eng.scalar_tensor_tensor(dA[:, :], dA[:, :], c6,
                                                     dC[:, :],
                                                     OP.logical_shift_left,
                                                     OP.bitwise_or)
                            return eng.tensor_tensor(
                                wd_bf[:, sl], dA[:, :].bitcast(BF16),
                                scl3[:, sl], OP.mult)
                        return f
                    for ch in range(2):
                        s.add("vector", mk(ch),
                              deps=deps + ["ld.sp", "ld.i16"],
                              event=f"{tag}.d{ch}")
                    return [f"{tag}.d{ch}" for ch in range(2)]

                for dtg in range(4):
                    dsl = slice(512 * dtg, 512 * dtg + 512)
                    prev_dec = None
                    for kcg in range(8):
                        ctag = f"C{dtg}_{kcg}"
                        pdeps = [prev_dec] if prev_dec else (
                            [f"C{dtg-1}.ev"] if dtg else [])
                        for par in range(2):
                            ld(f"{ctag}.pk{par}",
                               bass.AP(pk3, par * 16 * 512,
                                       [[2 * 16 * 512, 64], [512, 16],
                                        [1, 512]]),
                               D["dp"][64 * 16 * kcg:64 * 16 * (kcg + 1),
                                       dsl].rearrange(
                                   "(kc r) n -> r kc n", r=64),
                               deps=pdeps)
                        for a in range(4):
                            ld(f"{ctag}.sc{a}",
                               bass.AP(scl3, a * 32 * 16 * 512,
                                       [[16 * 512, 32], [512, 16], [1, 512]]),
                               D["ds"].rearrange(
                                   "(kc four) n -> four kc n",
                                   four=4)[a][16 * kcg:16 * kcg + 16, dsl]
                               .unsqueeze(0).broadcast_to([32, 16, 512]),
                               deps=pdeps, eng_name="gpsimd")
                        ddec = ([f"{ctag}.pk{p}" for p in range(2)] +
                                [f"{ctag}.sc{a}" for a in range(4)])
                        if prev_dec:
                            ddec.append(prev_dec)
                        wdst_war = (f"C{dtg}_{kcg-1}b3.mm" if kcg else
                                    (f"C{dtg-1}.ev" if dtg else None))
                        if wdst_war:
                            ddec.append(wdst_war)
                        emit_dec3(ctag, ddec)
                        prev_dec = f"{ctag}.d1"

                        for ib in range(4):
                            rlo = 128 * (16 * kcg + 4 * ib)
                            ngsrc = 4 * kcg + ib
                            dint = []
                            if dtg == 0:
                                dint.append(f"B{ngsrc}.int")
                            if ib:
                                dint.append(f"{ctag}b{ib-1}.mm")
                            elif kcg:
                                dint.append(f"C{dtg}_{kcg-1}b3.mm")
                            elif dtg:
                                dint.append(f"C{dtg-1}.ev")
                            ld(f"{ctag}.i{ib}",
                               int4[:].rearrange("p (j t) -> p j t", t=TOK),
                               inter_d.ap()[rlo:rlo + 512, :].rearrange(
                                   "(j p) t -> p j t", p=P),
                               deps=dint)

                            def mk_cm(dtg=dtg, kcg=kcg, ib=ib):
                                def f(eng):
                                    last = None
                                    for jj in range(4):
                                        kcl = 4 * ib + jj
                                        for j in range(4):
                                            for th in range(2):
                                                pslot = 512 * (2 * j + th)
                                                last = eng.matmul(
                                                    ps_d[:, pslot:
                                                         pslot + 512],
                                                    wd_bf[:,
                                                          512 * kcl +
                                                          128 * j:
                                                          512 * kcl +
                                                          128 * j + 128],
                                                    int4[:,
                                                         TOK * jj + 512 * th:
                                                         TOK * jj + 512 * th
                                                         + 512],
                                                    start=(kcg == 0 and
                                                           kcl == 0),
                                                    stop=(kcg == 7 and
                                                          kcl == 15))
                                    return last
                                return f
                            s.add("tensor", mk_cm(),
                                  deps=[f"{ctag}.i{ib}", f"{ctag}.d0",
                                        f"{ctag}.d1"],
                                  event=f"{ctag}b{ib}.mm")

                    def mk_ev(dtg=dtg):
                        def f(eng):
                            last = None
                            for j in range(4):
                                for th in range(2):
                                    pslot = 512 * (2 * j + th)
                                    et = 4 * dtg + j
                                    xsl = x2[:, TOK * et + 512 * th:
                                             TOK * et + 512 * th + 512]
                                    last = eng.scalar_tensor_tensor(
                                        xsl, ps_d[:, pslot:pslot + 512], 0.0,
                                        xsl, OP.bypass, OP.add)
                            return last
                        return f
                    s.add("vector", mk_ev(),
                          deps=[f"C{dtg}_7b3.mm", "C.x2"],
                          event=f"C{dtg}.ev")

                s.add("sync", lambda eng: eng.dma_start(
                    out=xo.rearrange("(kt p) t -> p kt t", p=P),
                    in_=x2[:].rearrange("p (kt t) -> p kt t", t=TOK)),
                    deps=[f"C{d}.ev" for d in range(4)],
                    event="C.xout", dma=True)

        s.fence_all()
        blk = top.enter_context(nc.Block("l0ffn_main"))
        s.emit(blk)
    return nc





_l1_ye_ref = [None]


def prep_l1attn(inputs):
    Wq = np.asarray(inputs["L1_attn_Wq"])
    Wk = np.asarray(inputs["L1_attn_Wk"])
    Wv = np.asarray(inputs["L1_attn_Wv"])
    Wo = np.asarray(inputs["L1_attn_Wo"])
    WqT = np.ascontiguousarray(Wq.T).astype(np.float32)
    WkT = np.ascontiguousarray(Wk.T).astype(np.float32)
    WvT = np.ascontiguousarray(Wv.T).astype(np.float32)
    WoT = np.ascontiguousarray(Wo.T).astype(np.float32)
    C = _consts()
    lam = {k: np.asarray(inputs[f"L1_attn_lambda_{k}"]).reshape(64, 1)
           .astype(np.float32) for k in ("q1", "k1", "q2", "k2")}
    lnw = np.asarray(inputs["L1_attn_ln_w"]).reshape(128, 1).astype(np.float32)
    maps = []
    for c in range(8):
        u = c % 4
        maps.append({
            "wqt": np.ascontiguousarray(WqT[:, 512 * u:512 * u + 512]),
            "wkt": np.ascontiguousarray(WkT[:, 128 * u:128 * u + 128]),
            "wvt": np.ascontiguousarray(WvT[:, 128 * u:128 * u + 128]),
            "wot": WoT,
            "lq1": lam["q1"], "lk1": lam["k1"],
            "lq2": lam["q2"], "lk2": lam["k2"], "lnw": lnw,
            "ctile": C["ctile"], "ident": C["ident"], "identf": C["identf"],
        })
    return maps


def build_l1attn():
    nc = bass.Bass("TRN2", target_bir_lowering=False)
    D = {}
    for name, shape, dt in [
        ("x", [E, TOK], F32), ("wqt", [E, 512], F32), ("wkt", [E, 128], F32),
        ("wvt", [E, 128], F32), ("wot", [E, E], F32),
        ("lq1", [64, 1], F32), ("lk1", [64, 1], F32),
        ("lq2", [64, 1], F32), ("lk2", [64, 1], F32), ("lnw", [128, 1], F32),
        ("ctile", [P, P], F32), ("ident", [P, P], BF16),
        ("identf", [P, P], F32),
    ]:
        D[name] = nc.dram_tensor(name, shape, dt, kind="ExternalInput").ap()
    xo = nc.dram_tensor("xo", [E, TOK], F32, kind="ExternalOutput").ap()
    agi = nc.dram_tensor("agi", [E, TOK], BF16)
    ago = nc.dram_tensor("ago", [4 * E, TOK], BF16)
    ygi = nc.dram_tensor("ygi", [512, T4], BF16)
    ygo = nc.dram_tensor("ygo", [E, T4], BF16)

    with ExitStack() as top:
        s = Sched(nc, top)
        sb = lambda n_, sh, dt: top.enter_context(nc.sbuf_tensor(n_, sh, dt))
        ident = sb("sb_ident", [P, P], BF16)
        identf = sb("sb_identf", [P, P], F32)
        ctile = sb("sb_ctile", [P, P], F32)
        ones_cb = sb("ones_cb", [P, 1], BF16)
        ones_cf = sb("ones_cf", [P, 1], F32)
        ones_rf = sb("ones_rf", [1, P], F32)
        eps_c = sb("eps_c", [P, 1], F32)
        lam_c = sb("lam_c", [P, 1], F32)
        lnw_c = sb("lnw_c", [P, 1], F32)
        wq_bf = sb("wq_bf", [P, KT * 512], BF16)
        wk_bf = sb("wk_bf", [P, KT * 128], BF16)
        wv_bf = sb("wv_bf", [P, KT * 128], BF16)

        s.add("vector", lambda eng: (
            eng.memset(ones_cb[:], 1.0), eng.memset(ones_cf[:], 1.0),
            eng.memset(ones_rf[:], 1.0), eng.memset(eps_c[:], EPS),
        )[-1], event="consts")

        def ld(name, out_ap, in_ap, deps=(), eng_name="sync"):
            s.add(eng_name, (lambda o=out_ap, i=in_ap:
                             (lambda eng: eng.dma_start(out=o, in_=i)))(),
                  deps=deps, event=name, dma=True)

        ld("ld.ident", ident[:], D["ident"])
        ld("ld.identf", identf[:], D["identf"])
        ld("ld.ctile", ctile[:], D["ctile"])

        with ExitStack() as ph:
            sbp = lambda n_, sh, dt: ph.enter_context(nc.sbuf_tensor(n_, sh, dt))
            x_sb = sbp("x_sb", [P, KT * TOK], F32)
            wst = sbp("wst", [P, KT * 512], F32)
            sqb = sbp("sqb", [P, 2 * TOK], F32)
            rows = sbp("rows", [1, 6 * TOK], F32)
            yh = sbp("yh_own", [P, KT * TOK], BF16)
            asc = sbp("asc", [P, 512], F32)
            lamv = sbp("lamv", [64, 4], F32)
            lrow = sbp("lrow", [1, 8], F32)
            with ExitStack() as php:
                psp = lambda n_, sh: php.enter_context(
                    nc.psum_tensor(n_, sh, F32))
                ps_s = psp("ps_s", [1, TOK])
                ps_q = psp("ps_q", [1, TOK])
                ps_bc = psp("ps_bc", [P, 1024])
                ps_l = psp("ps_l", [P, 64])

                ld("ld.x", x_sb[:].rearrange("p (kt t) -> p kt t", t=TOK),
                   D["x"].rearrange("(kt p) t -> p kt t", p=P))
                ld("ld.wq", wst[:].rearrange("p (kt n) -> p kt n", n=512),
                   D["wqt"].rearrange("(kt p) n -> p kt n", p=P))
                def cast_wq(eng):
                    last = None
                    for kt in range(KT):
                        last = eng.tensor_single_scalar(
                            wq_bf[:, 512 * kt:512 * kt + 512],
                            wst[:, 512 * kt:512 * kt + 512], RSD1, OP.mult)
                    return last
                s.add("vector", cast_wq, deps=["ld.wq"], event="w.q")
                ld("ld.wk", wst[:, 0:KT * 128].rearrange(
                    "p (kt n) -> p kt n", n=128),
                   D["wkt"].rearrange("(kt p) n -> p kt n", p=P),
                   deps=["w.q"])
                ld("ld.wv", wst[:, KT * 128:KT * 256].rearrange(
                    "p (kt n) -> p kt n", n=128),
                   D["wvt"].rearrange("(kt p) n -> p kt n", p=P),
                   deps=["w.q"])
                def cast_wkv(eng):
                    last = None
                    for kt in range(KT):
                        eng.tensor_copy(wk_bf[:, 128 * kt:128 * kt + 128],
                                        wst[:, 128 * kt:128 * kt + 128])
                        last = eng.tensor_copy(
                            wv_bf[:, 128 * kt:128 * kt + 128],
                            wst[:, KT * 128 + 128 * kt:
                                KT * 128 + 128 * kt + 128])
                    return last
                s.add("vector", cast_wkv, deps=["ld.wk", "ld.wv"],
                      event="w.kv")

                ld("ld.l1", lamv[:, 0:1], D["lq1"])
                ld("ld.l2", lamv[:, 1:2], D["lk1"])
                ld("ld.l3", lamv[:, 2:3], D["lq2"])
                ld("ld.l4", lamv[:, 3:4], D["lk2"])
                ld("ld.lnw", lnw_c[:], D["lnw"])

                def lam_mm(eng):
                    eng.matmul(ps_l[0:1, 0:1], lamv[0:64, 0:1],
                               lamv[0:64, 1:2], start=True, stop=True)
                    return eng.matmul(ps_l[0:1, 1:2], lamv[0:64, 2:3],
                                      lamv[0:64, 3:4], start=True, stop=True)
                s.add("tensor", lam_mm,
                      deps=["ld.l1", "ld.l2", "ld.l3", "ld.l4"],
                      event="lam.mm")
                s.add("scalar", lambda eng: eng.activation(
                    lrow[0:1, 0:2], ps_l[0:1, 0:2], AF.Exp),
                    deps=["lam.mm"], event="lam.exp")

                def lam_fin(eng):
                    eng.tensor_tensor(lrow[0:1, 2:3], lrow[0:1, 0:1],
                                      lrow[0:1, 1:2], OP.subtract)
                    return eng.tensor_scalar(
                        lrow[0:1, 2:3], lrow[0:1, 2:3], LAMBDA_INIT, -1.0,
                        OP.add, OP.mult)
                s.add("vector", lam_fin, deps=["lam.exp"], event="lam.fin")
                s.add("tensor", lambda eng: eng.matmul(
                    ps_l[:, 8:9], ones_rf[0:1, :], lrow[0:1, 2:3],
                    start=True, stop=True),
                    deps=["lam.fin", "consts"], event="lam.bc")
                s.add("vector", lambda eng: (
                    eng.tensor_copy(lam_c[:], ps_l[:, 8:9]),
                    eng.tensor_single_scalar(lnw_c[:], lnw_c[:],
                                             1.0 - LAMBDA_INIT, OP.mult),
                )[-1], deps=["lam.bc", "ld.lnw"], event="lam.done")

                r1, n1 = emit_ln(s, "ln", TOK,
                                 lambda kt: x_sb[:, TOK * kt:TOK * kt + TOK],
                                 True, sqb, rows, ps_s, ps_q,
                                 ones_cf[:], ones_cb[:], eps_c[0:1, 0:1],
                                 deps=["ld.x", "consts"])
                emit_ln_apply(s, "ln", TOK,
                              lambda kt: x_sb[:, TOK * kt:TOK * kt + TOK],
                              lambda kt: yh[:, TOK * kt:TOK * kt + TOK],
                              r1, n1, ps_bc, ones_rf[0:1, :], asc, deps=[])
                s.add("sync", lambda eng: eng.dma_start(
                    out=agi.ap().rearrange("(kt p) t -> p kt t", p=P),
                    in_=yh[:].rearrange("p (kt t) -> p kt t", t=TOK)),
                    deps=["ln.done"], event="agi.done", dma=True)
                s.add("gpsimd", lambda eng: eng.collective_compute(
                    "AllGather", OP.bypass, replica_groups=GROUPS,
                    ins=[agi.ap().opt()], outs=[ago.ap().opt()]),
                    deps=["agi.done"], event="ag1", coll=True)
                s.barrier()

        with ExitStack() as mid:
            sbm = lambda n_, sh, dt: mid.enter_context(
                nc.sbuf_tensor(n_, sh, dt))
            q_bf = sbm("q_bf", [P, 4 * T4], BF16)
            k_bf = sbm("k_bf", [P, T4], BF16)
            k_sw = sbm("k_sw", [P, T4], BF16)
            v_bf = sbm("v_bf", [P, 32 * 128], BF16)

            with ExitStack() as ph2:
                sb2 = lambda n_, sh, dt: ph2.enter_context(
                    nc.sbuf_tensor(n_, sh, dt))
                yhg = sb2("yhg", [P, KT * 512], BF16)
                with ExitStack() as ph2p:
                    ps2 = lambda n_, sh: ph2p.enter_context(
                        nc.psum_tensor(n_, sh, F32))
                    ps_qp = ps2("ps_qp", [P, 512])
                    ps_kp = ps2("ps_kp", [P, 512])
                    ps_vp = ps2("ps_vp", [P, 512])

                    for hb in range(8):
                        ld(f"p2.y{hb}",
                           yhg[:].rearrange("p (kt t) -> p kt t", t=512),
                           ago.ap()[E * (hb // 2):E * (hb // 2 + 1),
                                    512 * (hb % 2):512 * (hb % 2) + 512]
                           .rearrange("(kt p) t -> p kt t", p=P),
                           deps=["ag1"] + ([f"p2.hbdone{hb-1}"] if hb
                                           else []))
                        for j in range(4):
                            def mk_q(hb=hb, j=j):
                                def f(eng):
                                    last = None
                                    for kc in range(KT):
                                        last = eng.matmul(
                                            ps_qp[:, :],
                                            wq_bf[:, 512 * kc + 128 * j:
                                                  512 * kc + 128 * j + 128],
                                            yhg[:, 512 * kc:512 * kc + 512],
                                            start=(kc == 0),
                                            stop=(kc == KT - 1))
                                    return last
                                return f
                            dq_ = [f"p2.y{hb}", "w.q"]
                            if hb or j:
                                dq_.append(f"p2.qe{hb}_{j-1}" if j else
                                           f"p2.qe{hb-1}_3")
                            s.add("tensor", mk_q(), deps=dq_,
                                  event=f"p2.qm{hb}_{j}")
                            s.add("vector",
                                  (lambda hb=hb, j=j: (lambda eng:
                                   eng.tensor_copy(
                                       q_bf[:, T4 * j + 512 * hb:
                                            T4 * j + 512 * hb + 512],
                                       ps_qp[:, :])))(),
                                  deps=[f"p2.qm{hb}_{j}"],
                                  event=f"p2.qe{hb}_{j}")
                        def mk_k(hb=hb):
                            def f(eng):
                                last = None
                                for kc in range(KT):
                                    last = eng.matmul(
                                        ps_kp[:, :],
                                        wk_bf[:, 128 * kc:128 * kc + 128],
                                        yhg[:, 512 * kc:512 * kc + 512],
                                        start=(kc == 0), stop=(kc == KT - 1))
                                return last
                            return f
                        dk_ = [f"p2.y{hb}", "w.kv"]
                        if hb:
                            dk_.append(f"p2.ke{hb-1}")
                        s.add("tensor", mk_k(), deps=dk_, event=f"p2.km{hb}")
                        s.add("vector",
                              (lambda hb=hb: (lambda eng: eng.tensor_copy(
                                  k_bf[:, 512 * hb:512 * hb + 512],
                                  ps_kp[:, :])))(),
                              deps=[f"p2.km{hb}"], event=f"p2.ke{hb}")
                        for tt in range(4):
                            i = hb * 4 + tt

                            def mk_v(hb=hb, tt=tt):
                                def f(eng):
                                    last = None
                                    for kc in range(KT):
                                        last = eng.matmul(
                                            ps_vp[:, 0:128],
                                            yhg[:, 512 * kc + 128 * tt:
                                                512 * kc + 128 * tt + 128],
                                            wv_bf[:, 128 * kc:128 * kc + 128],
                                            start=(kc == 0),
                                            stop=(kc == KT - 1))
                                    return last
                                return f
                            dv_ = [f"p2.y{hb}", "w.kv"]
                            if i:
                                dv_.append(f"p2.ve{i-1}")
                            s.add("tensor", mk_v(), deps=dv_,
                                  event=f"p2.vm{i}")
                            s.add("vector",
                                  (lambda i=i: (lambda eng: eng.tensor_copy(
                                      v_bf[:, 128 * i:128 * i + 128],
                                      ps_vp[:, 0:128])))(),
                                  deps=[f"p2.vm{i}"], event=f"p2.ve{i}")
                        s.add("vector", lambda eng: eng.engine_nop(),
                              deps=[f"p2.qe{hb}_3", f"p2.ke{hb}",
                                    f"p2.ve{hb * 4 + 3}"],
                              event=f"p2.hbdone{hb}")
                    s.add("sync", lambda eng: eng.dma_start(
                        out=k_sw[0:64, :], in_=k_bf[64:128, :]),
                        deps=[f"p2.ke{h_}" for h_ in range(8)],
                        event="p2.ksw1", dma=True)
                    s.add("sync", lambda eng: eng.dma_start(
                        out=k_sw[64:128, :], in_=k_bf[0:64, :]),
                        deps=[f"p2.ke{h_}" for h_ in range(8)],
                        event="p2.ksw2", dma=True)
                    s.add("vector", lambda eng: eng.engine_nop(),
                          deps=["p2.hbdone7", "p2.ksw1", "p2.ksw2"],
                          event="p2.done")
                    s.barrier()

            with ExitStack() as ph3:
                sb3 = lambda n_, sh, dt: ph3.enter_context(
                    nc.sbuf_tensor(n_, sh, dt))
                p_bf = sb3("p_bf", [P, T4], BF16)
                pt_bf = sb3("pt_bf", [P, 32 * 512], BF16)
                sums = sb3("sums", [P, 8], F32)
                nmax = sb3("nmax", [P, 4], F32)
                srow = sb3("srow", [1, 512], F32)
                rrow = sb3("rrow", [1, 512], F32)
                rec_sb = sb3("rec_sb", [P, 512], F32)
                ye = sb3("ye", [P, T4], F32)
                _l1_ye_ref[0] = ye
                zc = sb3("zc", [P, 512], F32)
                zq = sb3("zq", [P, 512], F32)
                yo_bf = sb3("yo_bf", [P, 512], BF16)
                with ExitStack() as ph3p:
                    ps_sc = ph3p.enter_context(
                        nc.psum_tensor("ps_sc", [P, 8192], BF16))
                    sc32 = ps_sc[:].bitcast(F32)
                    ptv = pt_bf[:].rearrange("p (kb q) -> p kb q", q=512)

                    prev_tail = None
                    Lseq = 0
                    strp_hist = []
                    for pair in range(4):
                        for member in range(2):
                            h = 2 * pair + member
                            jt = pair
                            mb = 64 * member
                            kb_ = 64 * (h // 4)
                            ksrc = k_bf if kb_ == mb else k_sw
                            kb_ = mb
                            for stv in range(8):
                                nkb = 4 * (stv + 1)
                                for qi in range(4):
                                    qg = 4 * stv + qi
                                    kmax = 128 * (qg + 1)
                                    nch = (kmax + 511) // 512
                                    tag = f"a{h}_{qg}"

                                    def mk_qk(jt=jt, mb=mb, kb_=kb_, qg=qg,
                                              kmax=kmax, nch=nch, ksrc=ksrc):
                                        def f(eng):
                                            last = None
                                            for c in range(nch):
                                                n = min(512, kmax - 512 * c)
                                                last = eng.matmul(
                                                    sc32[:, 512 * c:
                                                         512 * c + n],
                                                    q_bf[mb:mb + 64,
                                                         T4 * jt + 128 * qg:
                                                         T4 * jt + 128 * qg +
                                                         128],
                                                    ksrc[kb_:kb_ + 64,
                                                         512 * c:512 * c + n],
                                                    start=True, stop=True)
                                            return last
                                        return f
                                    dd = ["p2.done", "ld.ctile"]
                                    if qg:
                                        dd.append(f"a{h}_{qg-1}.pev")
                                    elif Lseq:
                                        dd.append(f"a{Lseq-1}_31.pev")
                                    if prev_tail:
                                        dd.append(prev_tail)
                                    s.add("tensor", mk_qk(), deps=dd,
                                          event=f"{tag}.qk")

                                    def mk_mx(kmax=kmax, qi=qi):
                                        def f(eng):
                                            eng.tensor_tensor(
                                                sc32[:, kmax - 128:kmax],
                                                sc32[:, kmax - 128:kmax],
                                                ctile[:], OP.add)
                                            return eng.reduce_max(
                                                nmax[:, qi:qi + 1],
                                                sc32[:, 0:kmax],
                                                axis=AX.X, negate=True)
                                        return f
                                    s.add("vector", mk_mx(),
                                          deps=[f"{tag}.qk"],
                                          event=f"{tag}.mx")

                                    def mk_exp(kmax=kmax, qi=qi, stv=stv):
                                        def f(eng):
                                            so = 4 * (stv % 2) + qi
                                            return eng.activation(
                                                p_bf[:, 0:kmax],
                                                sc32[:, 0:kmax],
                                                AF.Exp,
                                                bias=nmax[:, qi:qi + 1],
                                                scale=1.0,
                                                accum_out=sums[:, so:so + 1])
                                        return f
                                    dde = [f"{tag}.mx"]
                                    if Lseq * 8 + stv >= 2 and \
                                            len(strp_hist) >= 2:
                                        dde.append(strp_hist[-2])
                                    s.add("scalar", mk_exp(), deps=dde,
                                          event=f"{tag}.exp")

                                    nkbq = kmax // 128
                                    ng = (nkbq + 3) // 4
                                    for g in range(ng):
                                        nn = min(4 * g + 4, nkbq) - 4 * g

                                        def mk_tr(g=g, nn=nn):
                                            def f(eng):
                                                last = None
                                                for j in range(nn):
                                                    sl = (1024 * (g % 2) +
                                                          512 + 128 * j)
                                                    last = eng.transpose(
                                                        ps_sc[:, sl:sl + 128],
                                                        p_bf[:,
                                                             128 * (4 * g + j):
                                                             128 * (4 * g + j)
                                                             + 128],
                                                        ident[:])
                                                return last
                                            return f
                                        ddt = [f"{tag}.exp", "ld.ident"]
                                        if g >= 2:
                                            ddt.append(f"{tag}.te{g-2}")
                                        s.add("tensor", mk_tr(), deps=ddt,
                                              event=f"{tag}.tr{g}")

                                        def mk_te(qi=qi, g=g, nn=nn):
                                            def f(eng):
                                                sl = 1024 * (g % 2) + 512
                                                return eng.tensor_copy(
                                                    ptv[:, 4 * g:4 * g + nn,
                                                        128 * qi:
                                                        128 * qi + 128],
                                                    ps_sc[:, sl:sl + 128 * nn]
                                                    .rearrange(
                                                        "p (a b) -> p a b",
                                                        b=128))
                                            return f
                                        s.add("vector", mk_te(),
                                              deps=[f"{tag}.tr{g}"],
                                              event=f"{tag}.te{g}")
                                    s.add("vector",
                                          lambda eng: eng.engine_nop(),
                                          deps=[f"{tag}.te{g}"
                                                for g in range(ng)],
                                          event=f"{tag}.pev")

                                tps = f"b{h}_{stv}"
                                qdone = [f"a{h}_{4 * stv + qi}.pev"
                                         for qi in range(4)]

                                def mk_pv(stv=stv, nkb=nkb):
                                    def f(eng):
                                        last = None
                                        for kb in range(nkb):
                                            j = kb - 4 * stv
                                            lo = 128 * j if j > 0 else 0
                                            last = eng.matmul(
                                                sc32[:, lo:512],
                                                v_bf[:, 128 * kb:
                                                     128 * kb + 128],
                                                ptv[:, kb, lo:512],
                                                start=(kb == 0),
                                                stop=(kb == nkb - 1))
                                        return last
                                    return f
                                s.add("tensor", mk_pv(), deps=qdone,
                                      event=f"{tps}.pv")

                                def mk_strp(stv=stv):
                                    def f(eng):
                                        last = None
                                        for qi in range(4):
                                            so = 4 * (stv % 2) + qi
                                            last = eng.transpose(
                                                sc32[0:1, 512 + 128 * qi:
                                                     512 + 128 * qi + 128],
                                                sums[:, so:so + 1],
                                                identf[:])
                                        return last
                                    return f
                                s.add("tensor", mk_strp(),
                                      deps=qdone + [f"{tps}.pv"],
                                      event=f"{tps}.str")
                                strp_hist.append(f"{tps}.str")
                                s.add("vector",
                                      (lambda: (lambda eng: eng.tensor_copy(
                                          srow[0:1, :],
                                          sc32[0:1, 512:1024])))(),
                                      deps=[f"{tps}.str"],
                                      event=f"{tps}.srow")
                                s.add("vector",
                                      (lambda: (lambda eng: eng.reciprocal(
                                          rrow[0:1, :], srow[0:1, :])))(),
                                      deps=[f"{tps}.srow"],
                                      event=f"{tps}.rrow")
                                s.add("tensor",
                                      (lambda: (lambda eng: eng.matmul(
                                          sc32[:, 512:1024], ones_rf[0:1, :],
                                          rrow[0:1, :], start=True,
                                          stop=True)))(),
                                      deps=[f"{tps}.rrow", f"{tps}.srow"],
                                      event=f"{tps}.bc")
                                s.add("vector",
                                      (lambda: (lambda eng: eng.tensor_copy(
                                          rec_sb[:, :],
                                          sc32[:, 512:1024])))(),
                                      deps=[f"{tps}.bc"], event=f"{tps}.rec")

                                if member == 0:
                                    def mk_ye(stv=stv):
                                        def f(eng):
                                            return eng.scalar_tensor_tensor(
                                                ye[:, 512 * stv:
                                                   512 * stv + 512],
                                                sc32[:, 0:512], 0.0,
                                                rec_sb[:, :], OP.bypass,
                                                OP.mult)
                                        return f
                                    dye = [f"{tps}.rec", f"{tps}.pv"]
                                    s.add("vector", mk_ye(), deps=dye,
                                          event=f"{tps}.tail")
                                else:
                                    def mk_z(stv=stv):
                                        def f(eng):
                                            eng.scalar_tensor_tensor(
                                                zc[:, :], sc32[:, 0:512],
                                                0.0, rec_sb[:, :],
                                                OP.bypass, OP.mult)
                                            return eng.scalar_tensor_tensor(
                                                zc[:, :], zc[:, :],
                                                lam_c[:, 0:1],
                                                ye[:, 512 * stv:
                                                   512 * stv + 512],
                                                OP.mult, OP.add)
                                        return f
                                    s.add("vector", mk_z(),
                                          deps=[f"{tps}.rec", f"{tps}.pv",
                                                "lam.done",
                                                f"b{h-1}_{stv}.tail"],
                                          event=f"{tps}.z")
                                    s.add("scalar",
                                          (lambda: (lambda eng:
                                           eng.activation(
                                               zq[:, :], zc[:, :],
                                               AF.Square)))(),
                                          deps=[f"{tps}.z"],
                                          event=f"{tps}.zq")
                                    s.add("tensor",
                                          (lambda: (lambda eng: eng.matmul(
                                              sc32[0:1, 1024:1536],
                                              ones_cf[:], zq[:, :],
                                              start=True, stop=True)))(),
                                          deps=[f"{tps}.zq"],
                                          event=f"{tps}.ssq")
                                    s.add("scalar",
                                          (lambda: (lambda eng:
                                           eng.activation(
                                               rrow[0:1, :],
                                               sc32[0:1, 1024:1536],
                                               AF.Sqrt,
                                               bias=eps_c[0:1, 0:1],
                                               scale=1.0 / 128.0)))(),
                                          deps=[f"{tps}.ssq"],
                                          event=f"{tps}.rms1")
                                    s.add("vector",
                                          (lambda: (lambda eng:
                                           eng.reciprocal(
                                               srow[0:1, :],
                                               rrow[0:1, :])))(),
                                          deps=[f"{tps}.rms1"],
                                          event=f"{tps}.rms2")
                                    s.add("tensor",
                                          (lambda: (lambda eng: eng.matmul(
                                              sc32[:, 1024:1536],
                                              ones_rf[0:1, :], srow[0:1, :],
                                              start=True, stop=True)))(),
                                          deps=[f"{tps}.rms2"],
                                          event=f"{tps}.rmb")
                                    s.add("vector",
                                          (lambda: (lambda eng:
                                           eng.tensor_copy(
                                               rec_sb[:, :],
                                               sc32[:, 1024:1536])))(),
                                          deps=[f"{tps}.rmb"],
                                          event=f"{tps}.rmc")

                                    def mk_yo(stv=stv):
                                        def f(eng):
                                            return eng.scalar_tensor_tensor(
                                                yo_bf[:, :], zc[:, :],
                                                lnw_c[:, 0:1], rec_sb[:, :],
                                                OP.mult, OP.mult)
                                        return f
                                    dyo = [f"{tps}.rmc", "lam.done"]
                                    if stv:
                                        dyo.append(f"b{h}_{stv-1}.yw")
                                    elif pair:
                                        dyo.append(f"b{2*pair-1}_7.yw")
                                    s.add("vector", mk_yo(), deps=dyo,
                                          event=f"{tps}.yo")
                                    s.add("sync",
                                          (lambda pair=pair, stv=stv:
                                           (lambda eng: eng.dma_start(
                                               out=ygi.ap()[
                                                   128 * pair:
                                                   128 * pair + 128,
                                                   512 * stv:
                                                   512 * stv + 512],
                                               in_=yo_bf[:, :])))(),
                                          deps=[f"{tps}.yo"],
                                          event=f"{tps}.yw", dma=True)
                                    s.add("vector",
                                          lambda eng: eng.engine_nop(),
                                          deps=[f"{tps}.yw"],
                                          event=f"{tps}.tail")
                                prev_tail = f"{tps}.tail"
                            Lseq += 1
                    s.barrier()

        L1DBG = os.environ.get("L1DBG", "")
        if L1DBG == "ye":
            s.add("gpsimd", lambda eng: eng.dma_start(
                out=xo[0:512, :].rearrange("(a p) t -> p a t", p=P),
                in_=_l1_ye_ref[0][:].rearrange("p (a t) -> p a t", t=TOK)),
                deps=["b6_7.tail"], event="dbgye", dma=True)
        if L1DBG == "yg":
            for a_ in range(4):
                s.add("gpsimd",
                      (lambda a_=a_: (lambda eng: eng.dma_start(
                          out=xo[512 * a_:512 * a_ + 512, :].rearrange(
                              "(b p) t -> p b t", p=P),
                          in_=ygi.ap()[128 * a_:128 * a_ + 128, :]
                          .rearrange("p (b t) -> p b t", t=TOK))))(),
                      deps=[f"b{2 * a_ + 1}_{stv}.yw" for stv in range(8)],
                      event=f"dbgyg{a_}", dma=True)
        s.add("gpsimd", lambda eng: eng.collective_compute(
            "AllGather", OP.bypass, replica_groups=GROUPS,
            ins=[ygi.ap().opt()], outs=[ygo.ap().opt()]),
            deps=[f"b{2 * pr + 1}_{stv}.yw" for pr in range(4)
                  for stv in range(8)],
            event="ag2", coll=True)

        with ExitStack() as ph4:
            sb4 = lambda n_, sh, dt: ph4.enter_context(
                nc.sbuf_tensor(n_, sh, dt))
            ya = sb4("ya", [P, KT * TOK], BF16)
            x2 = sb4("x2", [P, KT * TOK], F32)
            wof = sb4("wof", [P, KT * 256], F32)
            wob = sb4("wob", [P, KT * 256], BF16)
            with ExitStack() as ph4p:
                ps_o = [ph4p.enter_context(
                    nc.psum_tensor(f"ps_o{i}", [P, 512], F32))
                    for i in range(4)]

                def ya_load(eng):
                    for arm in nc.switch_core_id(engines=[eng]):
                        u = arm.logical % 4
                        eng.dma_start(
                            out=ya[:].rearrange("p (kt t) -> p kt t", t=TOK),
                            in_=ygo.ap()[:, TOK * u:TOK * u + TOK].rearrange(
                                "(kt p) t -> p kt t", p=P)).then_inc(
                            s.sems["sync"], 16)
                    return None
                s.add("sync", ya_load, deps=["ag2"], self_inc=16)
                s.events["p4.ya"] = ("sync", s.counts["sync"])

                ld("p4.x2", x2[:].rearrange("p (kt t) -> p kt t", t=TOK),
                   D["x"].rearrange("(kt p) t -> p kt t", p=P))

                for etg in range(8):
                    ld(f"p4.wo{etg}",
                       wof[:].rearrange("p (kt n) -> p kt n", n=256),
                       D["wot"][:, 256 * etg:256 * etg + 256].rearrange(
                           "(kt p) n -> p kt n", p=P),
                       deps=[f"p4.cast{etg-1}"] if etg else [])
                    ddc = [f"p4.wo{etg}"]
                    if etg:
                        ddc += [f"p4.om{2 * (etg - 1) + e2}_{t2}"
                                for e2 in range(2) for t2 in range(2)]
                    s.add("vector",
                          (lambda: (lambda eng: eng.tensor_copy(
                              wob[:], wof[:])))(),
                          deps=ddc, event=f"p4.cast{etg}")

                    for et2 in range(2):
                        et = 2 * etg + et2
                        for th in range(2):
                            pi = 2 * et2 + th

                            def mk_om(et2=et2, th=th, pi=pi):
                                def f(eng):
                                    last = None
                                    for jc in range(KT):
                                        last = eng.matmul(
                                            ps_o[pi][:, :],
                                            wob[:, 256 * jc + 128 * et2:
                                                256 * jc + 128 * et2 + 128],
                                            ya[:, TOK * jc + 512 * th:
                                               TOK * jc + 512 * th + 512],
                                            start=(jc == 0),
                                            stop=(jc == KT - 1))
                                    return last
                                return f
                            dd = [f"p4.cast{etg}", "p4.ya"]
                            if etg:
                                dd.append(f"p4.oev{2 * (etg - 1) + et2}_{th}")
                            s.add("tensor", mk_om(), deps=dd,
                                  event=f"p4.om{et}_{th}")

                            def mk_oev(et=et, th=th, pi=pi):
                                def f(eng):
                                    xsl = x2[:, TOK * et + 512 * th:
                                             TOK * et + 512 * th + 512]
                                    return eng.scalar_tensor_tensor(
                                        xsl, ps_o[pi][:, :], 0.0, xsl,
                                        OP.bypass, OP.add)
                                return f
                            s.add("vector", mk_oev(),
                                  deps=[f"p4.om{et}_{th}", "p4.x2"],
                                  event=f"p4.oev{et}_{th}")

                if not L1DBG:
                    s.add("sync", lambda eng: eng.dma_start(
                        out=xo.rearrange("(kt p) t -> p kt t", p=P),
                        in_=x2[:].rearrange("p (kt t) -> p kt t", t=TOK)),
                        deps=[f"p4.oev{et}_{th}" for et in range(16)
                              for th in range(2)],
                        event="p4.xout", dma=True)

        s.fence_all()
        blk = top.enter_context(nc.Block("l1attn_main"))
        s.emit(blk)
    return nc


def run_l1attn(inputs, x_fm_percore):
    maps = prep_l1attn(inputs)
    for c in range(8):
        maps[c]["x"] = x_fm_percore[c]
    nc = _get_nc("l1attn", build_l1attn)
    res = run_bass_kernel_spmd(nc, maps, core_ids=list(range(8)))
    return [np.asarray(res.results[c]["xo"]) for c in range(8)]





HID1 = 5632
NT1 = HID1 // 128
NG1 = HID1 // 512


def prep_l1ffn(inputs):
    g8 = _u8view(np.asarray(inputs["L1_ffn_wq_gate"]).T)
    u8_ = _u8view(np.asarray(inputs["L1_ffn_wq_up"]).T)
    d8 = _u8view(np.asarray(inputs["L1_ffn_wq_down"]).T)
    sg = np.ascontiguousarray(
        np.asarray(inputs["L1_ffn_s_gate"]).reshape(NT1, 128).T
    ).astype(np.float32)
    su = np.ascontiguousarray(
        np.asarray(inputs["L1_ffn_s_up"]).reshape(NT1, 128).T
    ).astype(np.float32)
    sd = np.ascontiguousarray(
        np.asarray(inputs["L1_ffn_s_down"]).reshape(KT, 128).T
    ).astype(np.float32)
    return {"g8": g8, "u8_": u8_, "d8": d8, "sg": sg, "su": su, "sd": sd}


def build_l1ffn():
    nc = bass.Bass("TRN2", target_bir_lowering=False)
    D = {}
    for name, shape, dt in [
        ("x", [E, TOK], F32),
        ("g8", [E, HID1], U8), ("u8_", [E, HID1], U8), ("d8", [HID1, E], U8),
        ("sg", [P, NT1], F32), ("su", [P, NT1], F32), ("sd", [P, KT], F32),
    ]:
        D[name] = nc.dram_tensor(name, shape, dt, kind="ExternalInput").ap()
    xo = nc.dram_tensor("xo", [E, TOK], F32, kind="ExternalOutput").ap()
    inter_d = nc.dram_tensor("inter_d", [HID1, TOK], BF16)

    with ExitStack() as top:
        s = Sched(nc, top)
        sb = lambda n_, sh, dt: top.enter_context(nc.sbuf_tensor(n_, sh, dt))
        ones_cb = sb("ones_cb", [P, 1], BF16)
        ones_cf = sb("ones_cf", [P, 1], F32)
        ones_rf = sb("ones_rf", [1, P], F32)
        eps_c = sb("eps_c", [P, 1], F32)
        sg_c = sb("sg_c", [P, NT1], F32)
        su_c = sb("su_c", [P, NT1], F32)
        sd_c = sb("sd_c", [P, KT], F32)
        xn = sb("xn", [P, KT * TOK], BF16)

        s.add("vector", lambda eng: (
            eng.memset(ones_cb[:], 1.0), eng.memset(ones_cf[:], 1.0),
            eng.memset(ones_rf[:], 1.0), eng.memset(eps_c[:], EPS),
        )[-1], event="consts")

        def ld(name, out_ap, in_ap, deps=(), eng_name="sync"):
            s.add(eng_name, (lambda o=out_ap, i=in_ap:
                             (lambda eng: eng.dma_start(out=o, in_=i)))(),
                  deps=deps, event=name, dma=True)

        ld("ld.sg", sg_c[:], D["sg"])
        ld("ld.su", su_c[:], D["su"])
        ld("ld.sd", sd_c[:], D["sd"])

        with ExitStack() as ph:
            sbp = lambda n_, sh, dt: ph.enter_context(nc.sbuf_tensor(n_, sh, dt))
            x_sb = sbp("x_sb", [P, KT * TOK], F32)
            sqb = sbp("sqb", [P, 2 * TOK], F32)
            rows = sbp("rows", [1, 6 * TOK], F32)
            asc = sbp("asc", [P, 512], F32)
            with ExitStack() as php:
                psp = lambda n_, sh: php.enter_context(
                    nc.psum_tensor(n_, sh, F32))
                ps_s = psp("ps_s", [1, TOK])
                ps_q = psp("ps_q", [1, TOK])
                ps_bc = psp("ps_bc", [P, 1024])
                ld("ld.x", x_sb[:].rearrange("p (kt t) -> p kt t", t=TOK),
                   D["x"].rearrange("(kt p) t -> p kt t", p=P))
                r1, n1 = emit_ln(s, "ln", TOK,
                                 lambda kt: x_sb[:, TOK * kt:TOK * kt + TOK],
                                 True, sqb, rows, ps_s, ps_q,
                                 ones_cf[:], ones_cb[:], eps_c[0:1, 0:1],
                                 deps=["ld.x", "consts"])
                emit_ln_apply(s, "ln", TOK,
                              lambda kt: x_sb[:, TOK * kt:TOK * kt + TOK],
                              lambda kt: xn[:, TOK * kt:TOK * kt + TOK],
                              r1, n1, ps_bc, ones_rf[0:1, :], asc, deps=[])
                s.barrier()

        with ExitStack() as ph2:
            sb2 = lambda n_, sh, dt: ph2.enter_context(
                nc.sbuf_tensor(n_, sh, dt))
            pk2 = sb2("pk2", [P, 16 * 512], U8)
            wg_bf = sb2("wg_bf", [P, 16 * 512], BF16)
            wu_bf = sb2("wu_bf", [P, 16 * 512], BF16)
            eg1 = sb2("eg1", [P, 512], F32)
            eg2 = sb2("eg2", [P, 512], F32)
            inter_t = sb2("inter_t", [P, 4 * TOK], BF16)
            with ExitStack() as ph2p:
                ps2 = lambda n_, sh: ph2p.enter_context(
                    nc.psum_tensor(n_, sh, F32))
                ps_g = ps2("ps_g", [P, 1024])
                ps_u = ps2("ps_u", [P, 1024])

                prev_mm_done = None
                prev_int_dma = None
                for ng in range(NG1):
                    nsl = slice(512 * ng, 512 * ng + 512)
                    for mat, (pkd, wdst) in enumerate(
                            ((D["g8"], wg_bf), (D["u8_"], wu_bf))):
                        mtag = f"B{ng}m{mat}"
                        pdeps = []
                        if mat == 1:
                            pdeps = [f"B{ng}m0.d"]
                        elif ng:
                            pdeps = [f"B{ng-1}m1.d"]
                        ld(f"{mtag}.pk",
                           pk2[:].rearrange("p (kc n) -> p kc n", n=512),
                           pkd[:, nsl].rearrange("(kc p) n -> p kc n", p=P),
                           deps=pdeps)
                        ddec = [f"{mtag}.pk"]
                        if prev_mm_done:
                            ddec.append(prev_mm_done)
                        s.add("vector",
                              (lambda wdst=wdst: (lambda eng:
                               eng.tensor_single_scalar(
                                   wdst[:], pk2[:], -128.0, OP.add)))(),
                              deps=ddec, event=f"{mtag}.d")

                    for n128 in range(4):
                        nt = 4 * ng + n128
                        for th in range(2):
                            ttag = f"B{ng}t{n128}_{th}"

                            def mk_mm(n128=n128, th=th):
                                def f(eng):
                                    last = None
                                    for kc in range(KT):
                                        rhs = xn[:, TOK * kc + 512 * th:
                                                 TOK * kc + 512 * th + 512]
                                        eng.matmul(
                                            ps_g[:, 512 * th:512 * th + 512],
                                            wg_bf[:, 512 * kc + 128 * n128:
                                                  512 * kc + 128 * n128 + 128],
                                            rhs, start=(kc == 0),
                                            stop=(kc == KT - 1))
                                        last = eng.matmul(
                                            ps_u[:, 512 * th:512 * th + 512],
                                            wu_bf[:, 512 * kc + 128 * n128:
                                                  512 * kc + 128 * n128 + 128],
                                            rhs, start=(kc == 0),
                                            stop=(kc == KT - 1))
                                    return last
                                return f
                            dmm = [f"B{ng}m0.d", f"B{ng}m1.d", "ln.done"]
                            if n128 or th:
                                dmm.append(f"B{ng}t{n128}_{th-1}.ep" if th
                                           else f"B{ng}t{n128-1}_1.ep")
                            elif ng:
                                dmm.append(f"B{ng-1}t3_1.ep")
                            s.add("tensor", mk_mm(), deps=dmm,
                                  event=f"{ttag}.mm")

                            def mk_si(nt=nt, th=th):
                                def f(eng):
                                    return eng.activation(
                                        eg1[:, :],
                                        ps_g[:, 512 * th:512 * th + 512],
                                        AF.Silu, bias=0.0,
                                        scale=sg_c[:, nt:nt + 1])
                                return f
                            s.add("scalar", mk_si(),
                                  deps=[f"{ttag}.mm", "ld.sg"],
                                  event=f"{ttag}.si")

                            def mk_int(nt=nt, n128=n128, th=th):
                                def f(eng):
                                    eng.tensor_scalar(
                                        eg2[:, :],
                                        ps_u[:, 512 * th:512 * th + 512],
                                        su_c[:, nt:nt + 1], None, OP.mult)
                                    io = inter_t[:, TOK * n128 + 512 * th:
                                                 TOK * n128 + 512 * th + 512]
                                    return eng.tensor_tensor(
                                        io, eg1[:, :], eg2[:, :], OP.mult)
                                return f
                            dint = [f"{ttag}.si", "ld.su"]
                            if prev_int_dma:
                                dint.append(prev_int_dma)
                            s.add("vector", mk_int(), deps=dint,
                                  event=f"{ttag}.ep")
                    prev_mm_done = f"B{ng}t3_1.mm"
                    ld(f"B{ng}.int",
                       inter_d.ap()[512 * ng:512 * ng + 512, :].rearrange(
                           "(j p) t -> p j t", p=P),
                       inter_t[:].rearrange("p (j t) -> p j t", t=TOK),
                       deps=[f"B{ng}t{n1}_{t1}.ep" for n1 in range(4)
                             for t1 in range(2)])
                    prev_int_dma = f"B{ng}.int"
                s.barrier()

        with ExitStack() as ph3:
            sb3 = lambda n_, sh, dt: ph3.enter_context(
                nc.sbuf_tensor(n_, sh, dt))
            x2 = sb3("x2", [P, KT * TOK], F32)
            pk3 = sb3("pk3", [P, NT1 * 512], U8)
            wd_bf = sb3("wd_bf", [P, NT1 * 512], BF16)
            int4 = sb3("int4", [P, 4 * TOK], BF16)
            with ExitStack() as ph3p:
                ps_d = ph3p.enter_context(
                    nc.psum_tensor("ps_d", [P, 8 * 512], F32))

                ld("C.x2", x2[:].rearrange("p (kt t) -> p kt t", t=TOK),
                   D["x"].rearrange("(kt p) t -> p kt t", p=P))

                for dtg in range(4):
                    dsl = slice(512 * dtg, 512 * dtg + 512)
                    ctag = f"C{dtg}"
                    pdeps = [f"C{dtg-1}.d"] if dtg else []
                    wdst_war = f"C{dtg-1}b10.mm" if dtg else None
                    ld(f"{ctag}.pk",
                       pk3[:].rearrange("p (kc n) -> p kc n", n=512),
                       D["d8"][:, dsl].rearrange("(kc p) n -> p kc n", p=P),
                       deps=pdeps + ([wdst_war] if wdst_war else []))
                    s.add("vector",
                          (lambda: (lambda eng: eng.tensor_single_scalar(
                              wd_bf[:], pk3[:], -128.0, OP.add)))(),
                          deps=[f"{ctag}.pk"] +
                               ([wdst_war] if wdst_war else []),
                          event=f"{ctag}.d")

                    for ib in range(11):
                        rlo = 512 * ib
                        dint = []
                        if dtg == 0:
                            dint.append(f"B{ib}.int")
                        if ib:
                            dint.append(f"{ctag}b{ib-1}.mm")
                        elif dtg:
                            dint.append(f"C{dtg-1}.ev")
                        ld(f"{ctag}.i{ib}",
                           int4[:].rearrange("p (j t) -> p j t", t=TOK),
                           inter_d.ap()[rlo:rlo + 512, :].rearrange(
                               "(j p) t -> p j t", p=P),
                           deps=dint)

                        def mk_cm(dtg=dtg, ib=ib):
                            def f(eng):
                                last = None
                                for jj in range(4):
                                    kcl = 4 * ib + jj
                                    for j in range(4):
                                        for th in range(2):
                                            pslot = 512 * (2 * j + th)
                                            last = eng.matmul(
                                                ps_d[:, pslot:pslot + 512],
                                                wd_bf[:, 512 * kcl + 128 * j:
                                                      512 * kcl + 128 * j +
                                                      128],
                                                int4[:, TOK * jj + 512 * th:
                                                     TOK * jj + 512 * th +
                                                     512],
                                                start=(kcl == 0),
                                                stop=(kcl == NT1 - 1))
                                return last
                            return f
                        s.add("tensor", mk_cm(),
                              deps=[f"{ctag}.i{ib}", f"{ctag}.d"],
                              event=f"{ctag}b{ib}.mm")

                    def mk_ev(dtg=dtg):
                        def f(eng):
                            last = None
                            for j in range(4):
                                for th in range(2):
                                    pslot = 512 * (2 * j + th)
                                    et = 4 * dtg + j
                                    xsl = x2[:, TOK * et + 512 * th:
                                             TOK * et + 512 * th + 512]
                                    last = eng.scalar_tensor_tensor(
                                        xsl, ps_d[:, pslot:pslot + 512],
                                        sd_c[:, et:et + 1], xsl,
                                        OP.mult, OP.add)
                            return last
                        return f
                    s.add("vector", mk_ev(),
                          deps=[f"{ctag}b10.mm", "C.x2", "ld.sd"],
                          event=f"{ctag}.ev")

                s.add("sync", lambda eng: eng.dma_start(
                    out=xo.rearrange("(kt p) t -> p kt t", p=P),
                    in_=x2[:].rearrange("p (kt t) -> p kt t", t=TOK)),
                    deps=[f"C{d}.ev" for d in range(4)],
                    event="C.xout", dma=True)
        s.fence_all()
        blk = top.enter_context(nc.Block("l1ffn_main"))
        s.emit(blk)
    return nc


def run_l1ffn(inputs, x_fm_percore):
    common = prep_l1ffn(inputs)
    maps = [dict(common, x=x_fm_percore[c]) for c in range(8)]
    nc = _get_nc("l1ffn", build_l1ffn)
    res = run_bass_kernel_spmd(nc, maps, core_ids=list(range(8)))
    return [np.asarray(res.results[c]["xo"]) for c in range(8)]


_NC_CACHE = {}


def _get_nc(name, builder):
    if name not in _NC_CACHE:
        _NC_CACHE[name] = builder()
    return _NC_CACHE[name]


def run_l0attn(inputs):
    maps = prep_l0attn(inputs)
    nc = _get_nc("l0attn", build_l0attn)
    res = run_bass_kernel_spmd(nc, maps, core_ids=list(range(8)))
    return [np.asarray(res.results[c]["xo"]) for c in range(8)]


def run_l0ffn(inputs, x_fm_percore):
    common = prep_l0ffn(inputs)
    maps = [dict(common, x=x_fm_percore[c]) for c in range(8)]
    nc = _get_nc("l0ffn", build_l0ffn)
    res = run_bass_kernel_spmd(nc, maps, core_ids=list(range(8)))
    return [np.asarray(res.results[c]["xo"]) for c in range(8)]


def kernel(**inputs):
    outs = run_l0attn(inputs)
    outs = run_l0ffn(inputs, outs)
    outs = run_l1attn(inputs, outs)
    outs = run_l1ffn(inputs, outs)
    res = np.zeros((2, 4096, 2048), np.float32)
    for c in range(8):
        b, u = c // 4, c % 4
        res[b, 1024 * u:1024 * u + 1024, :] = outs[c].T
    return res


if __name__ == "__main__":
    nc = build_l0attn()
    n = sum(1 for _ in nc.all_instructions())
    print("l0attn builds,", n, "instructions")
